# version 01 (final)
# baseline (speedup 1.0000x reference)
import contextlib
import numpy as np
import ml_dtypes

import concourse.bass as bass
import concourse.mybir as mybir
from concourse.bass_utils import run_bass_kernel_spmd

BF = ml_dtypes.bfloat16
F32 = mybir.dt.float32
BF16 = mybir.dt.bfloat16
I32 = mybir.dt.int32
AF = mybir.ActivationFunctionType
ALU = mybir.AluOpType
AX = mybir.AxisListType

T = 2048
D = 1024
NT = T // 128
NC = D // 128

ENGINES = ("sync", "tensor", "vector", "scalar", "gpsimd")
SEMS = ("w", "g", "pe", "act", "dve", "pool", "out")

ALL_LAYERS = ("l0_attn", "l0_ffn", "l1_attn", "l1_ffn",
              "l2_attn", "l2_ffn", "l3_attn", "l3_ffn")


class K:

    def __init__(self, nc):
        self.nc = nc
        self.ops = {e: [] for e in ENGINES}
        self.cnt = {s: 0 for s in SEMS}
        self.maxwait = {s: 0 for s in SEMS}
        self.dmagate = {s: 0 for s in SEMS}
        self.sems = {}
        self.t = {}

    AUTO = {"vector": "dve", "scalar": "act", "gpsimd": "pool"}

    def emit(self, eng, fn, inc=None):
        if eng in self.AUTO and inc is None:
            s = self.AUTO[eng]
            cur = self.cnt[s]
            if cur > 0:
                self.ops[eng].append(
                    (lambda e, ss=s, v=cur: e.wait_ge(self.sems[ss], v), None))
            self.cnt[s] += 1
            self.ops[eng].append((fn, (s, 1)))
            return self.cnt[s]
        self.ops[eng].append((fn, inc))
        if inc is not None:
            self.cnt[inc[0]] += inc[1]
            return self.cnt[inc[0]]
        return None

    def wait(self, eng, sem, val):
        if val and val > 0:
            self.maxwait[sem] = max(self.maxwait[sem], val)
            self.ops[eng].append(
                (lambda e, s=sem, v=val: e.wait_ge(self.sems[s], v), None))

    def dma(self, queue, out, in_, sem="w"):
        if self.maxwait[sem] > self.dmagate[sem]:
            self.dmagate[sem] = self.maxwait[sem]
            self.ops[queue].append(
                (lambda e, s=sem, v=self.dmagate[sem]:
                 e.wait_ge(self.sems[s], v), None))
        return self.emit(queue, lambda e, o=out, i=in_: e.dma_start(out=o, in_=i),
                         inc=(sem, 16))

    def sb(self, name, shape, dtype):
        h = self.nc.alloc_sbuf_tensor(name, shape, dtype)
        self.t[name] = h
        return h

    def arena_init(self, nbytes):
        h = self.nc.alloc_sbuf_tensor("arena", [128, nbytes], mybir.dt.uint8)
        self.t["arena"] = h
        self.arena_addr = self.nc.lookup_mloc(h).addr
        self.arena_size = nbytes
        self.arena_off = 0

    def arena_reset(self):
        self.arena_off = 0

    def sba(self, name, shape, dtype, at=None):
        per_part = int(np.prod(shape[1:])) * {F32: 4, BF16: 2, I32: 4,
                                              mybir.dt.uint8: 1}[dtype]
        off = ((self.arena_off + 63) & ~63) if at is None else at
        assert off + per_part <= self.arena_size, \
            f"arena overflow: {name} needs {per_part} at {off} of {self.arena_size}"
        h = self.nc.alloc_sbuf_tensor_at(name, shape, dtype,
                                         offset=self.arena_addr + off)
        if at is None:
            self.arena_off = off + per_part
        else:
            self.arena_off = max(self.arena_off, off + per_part)
        self.t[name] = h
        return h

    def sb_over(self, name, shape, dtype, base_name, byte_off=0):
        addr = self.nc.lookup_mloc(self.t[base_name]).addr + byte_off
        h = self.nc.alloc_sbuf_tensor_at(name, shape, dtype, offset=addr)
        self.t[name] = h
        return h

    def ps(self, name, shape, dtype=F32):
        h = self.nc.alloc_psum_tensor(name, shape, dtype)
        self.t[name] = h
        return h

    def replay(self, es):
        nc = self.nc
        for s in SEMS:
            self.sems[s] = es.enter_context(nc.semaphore(s))
        ops, sems = self.ops, self.sems

        def body(e, eng_name):
            for fn, inc in ops[eng_name]:
                ins = fn(e)
                if inc is not None:
                    ins.then_inc(sems[inc[0]], inc[1])

        with nc.Block() as blk:
            blk.sync(lambda e: body(e, "sync"))
            blk.tensor(lambda e: body(e, "tensor"))
            blk.vector(lambda e: body(e, "vector"))
            blk.scalar(lambda e: body(e, "scalar"))
            blk.gpsimd(lambda e: body(e, "gpsimd"))



def _t3(w):
    din, dout = w.shape
    return np.ascontiguousarray(w.reshape(din // 128, 128, dout))


def prep_weights(inp):
    d = {}
    d["c_ident_bf"] = np.eye(128, dtype=BF)
    d["c_ident_f"] = np.eye(128, dtype=np.float32)
    d["c_causal"] = np.where(np.arange(128)[None, :] <= np.arange(128)[:, None],
                             0.0, -1e30).astype(np.float32)
    d["c_onesrow"] = np.ones((1, 128), dtype=np.float32)
    d["l0_wq"] = _t3(inp["L0_attn_Wq"].T.copy())
    d["l0_wk"] = _t3(inp["L0_attn_Wk"].T.copy())
    d["l0_wv"] = _t3(inp["L0_attn_Wv"].T.copy())
    d["l0f_wg"] = _t3(np.ascontiguousarray(inp["L0_ffn_Wg"]))
    d["l0f_bg"] = np.ascontiguousarray(inp["L0_ffn_bg"].reshape(1, 16))
    d["l0f_bias"] = np.ascontiguousarray(inp["L0_ffn_bias"].reshape(1, 16))
    d["l0f_w1"] = np.ascontiguousarray(
        inp["L0_ffn_W1"].reshape(16, 8, 128, 2048))
    d["l0f_w2"] = np.ascontiguousarray(
        inp["L0_ffn_W2"].reshape(16, 16, 128, 1024))
    d["l0f_b1fm"] = np.ascontiguousarray(
        inp["L0_ffn_b1"].reshape(16, 16, 128).transpose(2, 0, 1).reshape(128, 256))
    d["l0f_b2"] = np.ascontiguousarray(inp["L0_ffn_b2"])
    wat = inp["L1_attn_w_attn"].T
    d["l1_wv"] = _t3(np.ascontiguousarray(wat[:, 2048:3072]))
    wqk = np.empty((16, 8, 128, 128), np.float32)
    for h in range(16):
        wqk[h, :, :, 0:64] = wat[:, h * 64:(h + 1) * 64].reshape(8, 128, 64)
        wqk[h, :, :, 64:128] = wat[:, 1024 + h * 64:1024 + (h + 1) * 64].reshape(8, 128, 64)
    d["l1_wqk"] = np.ascontiguousarray(wqk)
    ba = inp["L1_attn_b_attn"]
    bqk = np.empty((64, 32), np.float32)
    for h in range(16):
        bqk[:, 2 * h] = ba[h * 64:(h + 1) * 64]
        bqk[:, 2 * h + 1] = ba[1024 + h * 64:1024 + (h + 1) * 64]
    d["l1_bqk"] = np.ascontiguousarray(bqk)
    d["l1_bv"] = np.ascontiguousarray(ba[2048:3072].reshape(1, 1024))
    d["l1_wp"] = _t3(inp["L1_attn_w_proj"].T.copy())
    d["l1_bp"] = np.ascontiguousarray(inp["L1_attn_b_proj"].reshape(1, 1024))
    d["l1f_wr"] = _t3(inp["L1_ffn_W_r"].T.copy())
    d["l1f_wk"] = _t3(inp["L1_ffn_W_k"].T.copy())
    d["l1f_wv"] = _t3(inp["L1_ffn_W_v"].T.copy())
    d["l1f_mur"] = np.ascontiguousarray(inp["L1_ffn_mu_r"].reshape(8, 128).T)
    d["l1f_muk"] = np.ascontiguousarray(inp["L1_ffn_mu_k"].reshape(8, 128).T)
    pq = inp["L2_attn_P_q"].transpose(1, 0, 2).reshape(1024, 1024)
    pk = inp["L2_attn_P_k"].transpose(1, 0, 2).reshape(1024, 1024)
    pv = inp["L2_attn_P_v"].transpose(1, 0, 2).reshape(1024, 1024)
    d["l2_wv"] = _t3(np.ascontiguousarray(pv))
    wqk2 = np.empty((16, 8, 128, 128), np.float32)
    pq3, pk3 = pq.reshape(8, 128, 1024), pk.reshape(8, 128, 1024)
    for h in range(16):
        wqk2[h, :, :, 0:64] = pq3[:, :, h * 64:(h + 1) * 64]
        wqk2[h, :, :, 64:128] = pk3[:, :, h * 64:(h + 1) * 64]
    d["l2_wqk"] = np.ascontiguousarray(wqk2)
    d["l2_wp"] = _t3(np.ascontiguousarray(
        inp["L2_attn_P_o"].transpose(0, 2, 1).reshape(1024, 1024)))
    wgT = inp["L2_ffn_w_gate"].T
    wuT = inp["L2_ffn_w_up"].T
    wgu = np.empty((24, 8, 128, 256), np.int32)
    wg4, wu4 = wgT.reshape(8, 128, 3072), wuT.reshape(8, 128, 3072)
    for m in range(24):
        wgu[m, :, :, 0:128] = wg4[:, :, m * 128:(m + 1) * 128]
        wgu[m, :, :, 128:256] = wu4[:, :, m * 128:(m + 1) * 128]
    d["l2f_wgu"] = np.ascontiguousarray(wgu)
    d["l2f_wd"] = np.ascontiguousarray(
        inp["L2_ffn_w_down"].T.reshape(24, 128, 1024))
    d["l3_wqk"] = _t3(np.ascontiguousarray(inp["L3_attn_W_qk"]))
    d["l3_wv"] = _t3(np.ascontiguousarray(inp["L3_attn_W_v"]))
    d["l3_bqk"] = np.ascontiguousarray(inp["L3_attn_b_qk"].reshape(16, 64).T)
    d["l3_bv"] = np.ascontiguousarray(inp["L3_attn_b_v"].reshape(1, 1024))
    d["l3_wmass_flat"] = np.ascontiguousarray(
        inp["L3_attn_W_mass"].reshape(8, 128, 16).transpose(1, 0, 2).reshape(128, 128))
    d["l3_bm"] = np.ascontiguousarray(inp["L3_attn_b_mass"].reshape(1, 16))
    d["l3f_wr"] = _t3(np.ascontiguousarray(inp["L3_ffn_router_w"]))
    ug = np.empty((8, 32, 128, 8, 256), np.float32)
    dn = np.empty((8, 32, 128, 1024), np.float32)
    for e in range(8):
        if e < 7:
            u, g, dw = (inp["L3_ffn_up_w"][e], inp["L3_ffn_gate_w"][e],
                        inp["L3_ffn_down_w"][e])
        else:
            u, g, dw = (inp["L3_ffn_shared_up_w"], inp["L3_ffn_shared_gate_w"],
                        inp["L3_ffn_shared_down_w"])
        ug[e, :, :, :, 0:128] = u.reshape(8, 128, 32, 128).transpose(2, 1, 0, 3)
        ug[e, :, :, :, 128:256] = g.reshape(8, 128, 32, 128).transpose(2, 1, 0, 3)
        dn[e] = dw.reshape(32, 128, 1024)
    d["l3f_ug"] = np.ascontiguousarray(ug)
    d["l3f_dn"] = np.ascontiguousarray(dn)
    return d


def prep_x(inp):
    x = np.ascontiguousarray(inp["L0_attn_x"])
    return [np.ascontiguousarray(x[b]) for b in range(x.shape[0])]



def alloc_common(k):
    k.sb("R", [128, NT * 1024], F32)
    k.sb("xnT", [128, NC * T], BF16)
    k.sb("st_sum", [128, 16], F32)
    k.sb("st_sq", [128, 16], F32)
    k.sb("st_mean", [128, 16], F32)
    k.sb("st_m2", [128, 16], F32)
    k.sb("st_var", [128, 16], F32)
    k.sb("st_rstd", [128, 16], F32)
    k.sb("st_nmr", [128, 16], F32)
    k.sb("c_eps", [128, 1], F32)
    k.sb("sm_nm", [128, 1], F32)
    k.sb("sm_nm2", [128, 1], F32)
    k.sb("sm_sum", [128, 1], F32)
    k.sb("sm_rinv", [128, 1], F32)
    k.sb("E", [128, T], BF16)
    k.sb("ET", [128, T], BF16)
    k.sb("ident_bf", [128, 128], BF16)
    k.sb("causal", [128, 128], F32)
    k.sb("onesrow", [1, 128], F32)
    k.sb("onesrow_bf", [1, 128], BF16)
    k.sb("c_one", [128, 1], F32)
    k.sb("zero_row", [1, 512], BF16)
    k.sb("wstage", [128, 2048], F32)
    k.ps("PL", [128, 2048], F32)
    k.ps("PA", [128, 512], F32)
    k.ps("PB", [128, 512], F32)
    k.ps("PT", [128, 1024], BF16)
    k.ps("PX", [128, 512], F32)
    k.arena_init(93184)
    k.sba("xn_tm", [128, 1024], BF16, at=93184 - 2048)


def emit_prologue(k, aps):
    R = k.t["R"]
    for hh in range(4):
        k.dma("gpsimd",
              R.ap().rearrange("p (i d) -> p i d", d=1024)[:, hh * 4:(hh + 1) * 4],
              aps["x"].rearrange("(i p) d -> p i d", p=128)[:, hh * 4:(hh + 1) * 4],
              "g")
    for nm, src in (("ident_bf", "c_ident_bf"),
                    ("causal", "c_causal"), ("onesrow", "c_onesrow")):
        k.dma("gpsimd", k.t[nm].ap(), aps[src], "g")
    g0 = k.cnt["g"]
    k.wait("vector", "g", g0)
    k.emit("vector", lambda e: e.memset(k.t["c_eps"].ap(), 1e-5))
    k.emit("vector", lambda e: e.memset(k.t["c_one"].ap(), 1.0))
    k.emit("vector", lambda e: e.memset(k.t["zero_row"].ap(), 0))
    d0 = k.emit("vector", lambda e: e.tensor_copy(out=k.t["onesrow_bf"].ap(),
                                                  in_=k.t["onesrow"].ap()))
    k.wait("scalar", "g", g0)
    k.wait("tensor", "g", g0)
    k.wait("sync", "g", g0)
    return d0


def emit_epilogue(k, aps, r_ready):
    R = k.t["R"]
    k.wait("gpsimd", "dve", r_ready)
    for hh in range(4):
        k.dma("gpsimd",
              aps["out"].rearrange("(i p) d -> p i d", p=128)[:, hh * 4:(hh + 1) * 4],
              R.ap().rearrange("p (i d) -> p i d", d=1024)[:, hh * 4:(hh + 1) * 4],
              "out")
    k.wait("gpsimd", "out", k.cnt["out"])



def emit_ln(k, r_gate):
    R, xnT = k.t["R"], k.t["xnT"]
    sumc, sqc = k.t["st_sum"], k.t["st_sq"]
    scr = k.t["PL"]
    mean, m2 = k.t["st_mean"], k.t["st_m2"]
    var, rstd, nmr = k.t["st_var"], k.t["st_rstd"], k.t["st_nmr"]
    PT, xn = k.t["PT"], k.t["xn_tm"]
    k.wait("vector", "dve", r_gate)
    k.wait("scalar", "dve", r_gate)
    for i in range(NT):
        Ri = R.ap()[:, i * 1024:(i + 1) * 1024]
        k.emit("vector", lambda e, a=Ri, o=sumc.ap()[:, i:i + 1]:
               e.reduce_sum(out=o, in_=a, axis=AX.X))
        k.emit("scalar", lambda e, a=Ri, o=sqc.ap()[:, i:i + 1]:
               e.activation(out=scr.ap()[:, 0:1024], in_=a, func=AF.Square, accum_out=o))
    a_sq = k.cnt["act"]
    k.wait("vector", "act", a_sq)
    k.emit("vector", lambda e: e.tensor_scalar_mul(out=mean.ap(), in0=sumc.ap(),
                                                   scalar1=1.0 / D))
    k.emit("vector", lambda e: e.tensor_scalar_mul(out=var.ap(), in0=sqc.ap(),
                                                   scalar1=1.0 / D))
    k.emit("vector", lambda e: e.tensor_mul(out=m2.ap(), in0=mean.ap(), in1=mean.ap()))
    d_var = k.emit("vector", lambda e: e.tensor_sub(out=var.ap(), in0=var.ap(),
                                                    in1=m2.ap()))
    k.wait("scalar", "dve", d_var)
    a_std = k.emit("scalar", lambda e: e.activation(out=var.ap(), in_=var.ap(),
                                                    func=AF.Sqrt,
                                                    bias=k.t["c_eps"].ap(), scale=1.0))
    k.wait("vector", "act", a_std)
    k.emit("vector", lambda e: e.reciprocal(out=rstd.ap(), in_=var.ap()))
    k.emit("vector", lambda e: e.tensor_mul(out=nmr.ap(), in0=mean.ap(), in1=rstd.ap()))
    d_stats = k.emit("vector", lambda e: e.tensor_scalar_mul(out=nmr.ap(), in0=nmr.ap(),
                                                             scalar1=-1.0))
    k.wait("scalar", "dve", d_stats)
    for i in range(NT):
        a_app = k.emit("scalar", lambda e, a=R.ap()[:, i * 1024:(i + 1) * 1024],
                       s=rstd.ap()[:, i:i + 1], b=nmr.ap()[:, i:i + 1]:
                       e.activation(out=xn.ap(), in_=a, func=AF.Identity, bias=b, scale=s))
        k.wait("tensor", "act", a_app)
        for c in range(NC):
            k.emit("tensor", lambda e, src=xn.ap()[:, c * 128:(c + 1) * 128],
                   dst=PT.ap()[:, c * 128:(c + 1) * 128]:
                   e.transpose(dst, src, k.t["ident_bf"].ap()),
                   inc=("pe", 1) if c == NC - 1 else None)
        p_tr = k.cnt["pe"]
        k.wait("scalar", "pe", p_tr)
        dst = xnT.ap().rearrange("p (c t) -> p c t", c=NC)[:, :, i * 128:(i + 1) * 128]
        src = PT.ap().rearrange("p (c t) -> p c t", c=NC)
        k.emit("scalar", lambda e, s=src, d=dst: e.copy(out=d, in_=s))
    return k.cnt["act"]



def alloc_l0_attn(k):
    k.arena_reset()
    k.sba("l0_wqbf", [128, 4096], BF16)
    k.sba("l0_wkbf", [128, 4096], BF16)
    k.sba("l0_wvbf", [128, 8192], BF16)
    k.sba("l0_q", [64, T], BF16)
    k.sba("l0_k", [64, T], BF16)
    k.sba("l0_v", [128, NT * 1024], BF16)


def emit_l0_attn(k, aps, xn_act):
    nc = k.nc
    xnT = k.t["xnT"]
    stage, wqbf, wkbf, wvbf = (k.t["wstage"], k.t["l0_wqbf"],
                               k.t["l0_wkbf"], k.t["l0_wvbf"])
    q_sb, k_sb, v_sb = k.t["l0_q"], k.t["l0_k"], k.t["l0_v"]
    E, ET = k.t["E"], k.t["ET"]
    nm, ssum, rinv = k.t["sm_nm"], k.t["sm_sum"], k.t["sm_rinv"]
    R = k.t["R"]
    PL, PA, PB, PT, PX = k.t["PL"], k.t["PA"], k.t["PB"], k.t["PT"], k.t["PX"]

    po_prev = 0
    for wbf, src in ((wqbf, "l0_wq"), (wkbf, "l0_wk")):
        for r in range(2):
            k.wait("sync", "pool", po_prev)
            for c in range(4 * r, 4 * r + 4):
                k.dma("sync", stage.ap()[:, (c % 4) * 512:((c % 4) + 1) * 512],
                      aps[src][c])
            k.wait("gpsimd", "w", k.cnt["w"])
            po_prev = k.emit("gpsimd", lambda e, w2=wbf, rr=r:
                             e.tensor_copy(out=w2.ap()[:, rr * 2048:(rr + 1) * 2048],
                                           in_=stage.ap()))
    po_k = po_prev

    po_wv = {}
    last_wv_user_pe = {}

    def load_wv(h, war_pool_gate):
        gate = war_pool_gate
        for r in range(4):
            k.wait("sync", "pool", gate)
            for c in (2 * r, 2 * r + 1):
                k.dma("sync", stage.ap()[:, (c % 2) * 1024:((c % 2) + 1) * 1024],
                      aps["l0_wv"][c][:, h * 1024:(h + 1) * 1024])
            k.wait("gpsimd", "w", k.cnt["w"])
            if r == 0 and h - 1 in last_wv_user_pe:
                k.wait("gpsimd", "pe", last_wv_user_pe[h - 1])
            gate = k.emit("gpsimd", lambda e, rr=r:
                          e.tensor_copy(out=wvbf.ap()[:, rr * 2048:(rr + 1) * 2048],
                                        in_=stage.ap()))
        po_wv[h] = gate

    load_wv(0, po_k)

    k.wait("tensor", "act", xn_act)
    k.wait("tensor", "pool", po_k)

    a_prev_exp = 0
    d_prev_radd = 0
    a_prev_ptevac = 0

    for h in range(8):
        for (wbf, dst) in ((wqbf, q_sb), (wkbf, k_sb)):
            k.wait("tensor", "act", a_prev_exp)
            for c in range(NC):
                for t4 in range(4):
                    k.emit("tensor", lambda e,
                           l=wbf.ap()[:, c * 512 + h * 64: c * 512 + h * 64 + 64],
                           r=xnT.ap()[:, c * T + t4 * 512: c * T + (t4 + 1) * 512],
                           o=PL.ap()[0:64, t4 * 512:(t4 + 1) * 512],
                           st=(c == 0), sp=(c == NC - 1):
                           e.matmul(o, l, r, start=st, stop=sp),
                           inc=("pe", 1) if (c == NC - 1 and t4 == 3) else None)
            p_qk = k.cnt["pe"]
            k.wait("scalar", "pe", p_qk)
            a_prev_exp = k.emit("scalar", lambda e, d=dst:
                                e.copy(out=d.ap(), in_=PL.ap()[0:64, :]))
        k.wait("tensor", "pool", po_wv[h])
        k.wait("tensor", "dve", d_prev_radd)
        a_v_evac = {}
        for i in range(NT):
            if i % 2 == 0:
                outs = (PA.ap(), PB.ap())
            else:
                outs = (PX.ap(), PL.ap()[:, 0:512])
            if i >= 2:
                k.wait("tensor", "act", a_v_evac[i - 2])
            if i % 2 == 1:
                k.wait("tensor", "act", a_prev_exp)
            for c in range(NC):
                for n in range(2):
                    k.emit("tensor", lambda e,
                           l=xnT.ap()[:, c * T + i * 128: c * T + (i + 1) * 128],
                           r=wvbf.ap()[:, c * 1024 + n * 512: c * 1024 + (n + 1) * 512],
                           o=outs[n], st=(c == 0), sp=(c == NC - 1):
                           e.matmul(o, l, r, start=st, stop=sp),
                           inc=("pe", 1) if (c == NC - 1 and n == 1) else None)
            p_v = k.cnt["pe"]
            k.wait("scalar", "pe", p_v)
            k.emit("scalar", lambda e, s=outs[0], d=v_sb.ap()[:, i * 1024: i * 1024 + 512]:
                   e.copy(out=d, in_=s))
            a_v_evac[i] = k.emit("scalar", lambda e, s=outs[1],
                                 d=v_sb.ap()[:, i * 1024 + 512: (i + 1) * 1024]:
                                 e.copy(out=d, in_=s))
        last_wv_user_pe[h] = k.cnt["pe"]
        a_prev_exp = a_v_evac[NT - 1]
        if h + 1 < 8:
            load_wv(h + 1, po_wv[h])

        for i in range(NT):
            k.wait("tensor", "act", a_prev_exp)
            for kc in range(4):
                k.emit("tensor", lambda e,
                       l=q_sb.ap()[0:64, i * 128:(i + 1) * 128],
                       r=k_sb.ap()[0:64, kc * 512:(kc + 1) * 512],
                       o=PL.ap()[:, kc * 512:(kc + 1) * 512]:
                       e.matmul(o, l, r, start=True, stop=True),
                       inc=("pe", 1) if kc == 3 else None)
            p_log = k.cnt["pe"]
            k.wait("vector", "pe", p_log)
            d_max = k.emit("vector", lambda e: e.reduce_max(out=nm.ap(), in_=PL.ap(),
                                                            axis=AX.X, negate=True))
            k.wait("scalar", "dve", d_max)
            a_exp = k.emit("scalar", lambda e:
                           e.activation(out=E.ap(), in_=PL.ap(), func=AF.Exp,
                                        bias=nm.ap(), scale=1.0, accum_out=ssum.ap()))
            a_prev_exp = a_exp
            k.wait("vector", "act", a_exp)
            k.emit("vector", lambda e: e.reciprocal(out=rinv.ap(), in_=ssum.ap()))
            d_esc = k.emit("vector", lambda e:
                           e.tensor_scalar_mul(out=E.ap(), in0=E.ap(), scalar1=rinv.ap()))
            k.wait("tensor", "dve", d_esc)
            for r in range(2):
                k.wait("tensor", "act", a_prev_ptevac)
                for j in range(8):
                    jj = r * 8 + j
                    k.emit("tensor", lambda e,
                           src=E.ap()[:, jj * 128:(jj + 1) * 128],
                           dst=PT.ap()[:, j * 128:(j + 1) * 128]:
                           e.transpose(dst, src, k.t["ident_bf"].ap()),
                           inc=("pe", 1) if j == 7 else None)
                p_tr = k.cnt["pe"]
                k.wait("scalar", "pe", p_tr)
                a_prev_ptevac = k.emit("scalar", lambda e, rr=r:
                                       e.copy(out=ET.ap()[:, rr * 1024:(rr + 1) * 1024],
                                              in_=PT.ap()))
            k.wait("tensor", "act", a_prev_ptevac)
            k.wait("tensor", "dve", d_prev_radd)
            for j in range(NT):
                for n in range(2):
                    k.emit("tensor", lambda e,
                           l=ET.ap()[:, j * 128:(j + 1) * 128],
                           r=v_sb.ap()[:, j * 1024 + n * 512: j * 1024 + (n + 1) * 512],
                           o=(PA.ap() if n == 0 else PB.ap()),
                           st=(j == 0), sp=(j == NT - 1):
                           e.matmul(o, l, r, start=st, stop=sp),
                           inc=("pe", 1) if (j == NT - 1 and n == 1) else None)
            p_av = k.cnt["pe"]
            k.wait("vector", "pe", p_av)
            k.emit("vector", lambda e, d=R.ap()[:, i * 1024: i * 1024 + 512]:
                   e.tensor_add(out=d, in0=d, in1=PA.ap()))
            d_prev_radd = k.emit("vector", lambda e,
                                 d=R.ap()[:, i * 1024 + 512: (i + 1) * 1024]:
                                 e.tensor_add(out=d, in0=d, in1=PB.ap()))
    return d_prev_radd



BIG = np.float32(1e18)


def alloc_l0_ffn(k):
    k.arena_reset()
    k.sba("l0f_w1bf", [128, 16384], BF16)
    k.sba("l0f_w2bf", [128, 16384], BF16)
    k.sba("l0f_h", [128, 16 * 256], BF16)
    k.sba("l0f_wgbf", [128, 128], BF16)
    k.sba("l0f_bgb", [128, 16], F32)
    k.sba("l0f_b1fm", [128, 256], F32)
    k.sba("l0f_b2bf", [16, 1024], BF16)
    k.sba("l0f_lg", [128, 16], F32)
    k.sba("l0f_s1", [128, 16], F32)
    k.sba("l0f_s2", [128, 16], F32)
    k.sba("l0f_m1", [128, 1], F32)
    k.sba("l0f_m2", [128, 1], F32)
    k.sba("l0f_vd", [128, 1], F32)
    k.sba("l0f_w1c", [128, 1], F32)
    k.sba("l0f_w2c", [128, 1], F32)
    k.sba("l0f_cw", [128, 256], F32)
    k.sba("l0f_cwbf", [128, 256], BF16)
    k.sba("l0f_cwT", [16, 2048], BF16)
    k.sba("l0f_bgrow", [1, 16], F32)
    k.sba("l0f_birow", [1, 16], F32)


def emit_l0_ffn(k, aps, xn_act):
    xnT, R = k.t["xnT"], k.t["R"]
    stage = k.t["wstage"]
    w1bf, w2bf, hbf = k.t["l0f_w1bf"], k.t["l0f_w2bf"], k.t["l0f_h"]
    wgbf, bgb, b1fm, b2bf = (k.t["l0f_wgbf"], k.t["l0f_bgb"],
                             k.t["l0f_b1fm"], k.t["l0f_b2bf"])
    lg, s1, s2 = k.t["l0f_lg"], k.t["l0f_s1"], k.t["l0f_s2"]
    m1, m2, vd = k.t["l0f_m1"], k.t["l0f_m2"], k.t["l0f_vd"]
    w1c, w2c = k.t["l0f_w1c"], k.t["l0f_w2c"]
    cw, cwbf, cwT = k.t["l0f_cw"], k.t["l0f_cwbf"], k.t["l0f_cwT"]
    PL, PA, PB, PT, PX = k.t["PL"], k.t["PA"], k.t["PB"], k.t["PT"], k.t["PX"]
    one = k.t["c_one"]
    NB = 8
    BT = 256

    k.wait("sync", "act", xn_act)
    for c in range(NC):
        k.dma("sync", stage.ap()[:, c * 16:(c + 1) * 16], aps["l0f_wg"][c])
    k.dma("sync", k.t["l0f_bgrow"].ap(), aps["l0f_bg"])
    k.dma("sync", k.t["l0f_birow"].ap(), aps["l0f_bias"])
    k.dma("sync", b1fm.ap(), aps["l0f_b1fm"])
    k.dma("sync", stage.ap()[0:16, 1024:2048], aps["l0f_b2"])
    w_sm = k.cnt["w"]
    k.wait("gpsimd", "w", w_sm)
    po_wg = k.emit("gpsimd", lambda e: e.tensor_copy(out=wgbf.ap(),
                                                     in_=stage.ap()[:, 0:128]))
    po_b2 = k.emit("gpsimd", lambda e: e.tensor_copy(
        out=b2bf.ap(), in_=stage.ap()[0:16, 1024:2048]))

    k.wait("tensor", "act", xn_act)
    k.wait("tensor", "w", w_sm)
    k.emit("tensor", lambda e: e.matmul(PX.ap()[:, 0:16], k.t["onesrow"].ap(),
                                        k.t["l0f_bgrow"].ap(), start=True, stop=False))
    p_bc = k.emit("tensor", lambda e: e.matmul(PX.ap()[:, 0:16], k.t["onesrow"].ap(),
                                               k.t["l0f_birow"].ap(),
                                               start=False, stop=True), inc=("pe", 1))
    k.wait("scalar", "pe", p_bc)
    a_bgb = k.emit("scalar", lambda e: e.copy(out=bgb.ap(), in_=PX.ap()[:, 0:16]))

    k.wait("tensor", "pool", po_wg)
    k.wait("tensor", "act", a_bgb)
    d_lg = 0
    for i in range(NT):
        if d_lg:
            k.wait("tensor", "dve", d_lg)
        for c in range(NC):
            k.emit("tensor", lambda e,
                   l=xnT.ap()[:, c * T + i * 128: c * T + (i + 1) * 128],
                   r=wgbf.ap()[:, c * 16:(c + 1) * 16],
                   st=(c == 0), sp=(c == NC - 1):
                   e.matmul(PX.ap()[:, 0:16], l, r, start=st, stop=sp),
                   inc=("pe", 1) if c == NC - 1 else None)
        p_rt = k.cnt["pe"]
        k.wait("vector", "pe", p_rt)
        k.emit("vector", lambda e: e.tensor_add(out=lg.ap(), in0=PX.ap()[:, 0:16],
                                                in1=bgb.ap()))
        k.emit("vector", lambda e: e.reduce_max(out=m1.ap(), in_=lg.ap(), axis=AX.X))
        d_d = k.emit("vector", lambda e: e.tensor_scalar(
            out=s1.ap(), in0=lg.ap(), scalar1=m1.ap(), scalar2=None,
            op0=ALU.subtract))
        k.wait("scalar", "dve", d_d)
        a_m1 = k.emit("scalar", lambda e: e.activation(
            out=s2.ap(), in_=s1.ap(), func=AF.Relu, bias=one.ap(), scale=BIG))
        k.wait("vector", "act", a_m1)
        k.emit("vector", lambda e: e.scalar_tensor_tensor(
            out=s1.ap(), in0=s2.ap(), scalar=-1e18, in1=lg.ap(),
            op0=ALU.mult, op1=ALU.add))
        k.emit("vector", lambda e: e.reduce_max(out=m2.ap(), in_=s1.ap(), axis=AX.X))
        k.emit("vector", lambda e: e.tensor_sub(out=vd.ap(), in0=m2.ap(), in1=m1.ap()))
        d_s = k.emit("vector", lambda e: e.tensor_scalar(
            out=s1.ap(), in0=s1.ap(), scalar1=m2.ap(), scalar2=None,
            op0=ALU.subtract))
        k.wait("scalar", "dve", d_s)
        k.emit("scalar", lambda e: e.activation(out=k.t["l0f_w2c"].ap(), in_=vd.ap(),
                                                func=AF.Sigmoid))
        a_m2 = k.emit("scalar", lambda e: e.activation(
            out=s1.ap(), in_=s1.ap(), func=AF.Relu, bias=one.ap(), scale=BIG))
        k.wait("vector", "act", a_m2)
        k.emit("vector", lambda e: e.tensor_scalar(
            out=k.t["l0f_w1c"].ap(), in0=k.t["l0f_w2c"].ap(), scalar1=-1.0,
            scalar2=1.0, op0=ALU.mult, op1=ALU.add))
        k.emit("vector", lambda e: e.tensor_scalar_mul(
            out=s2.ap(), in0=s2.ap(), scalar1=k.t["l0f_w1c"].ap()))
        d_lg = k.emit("vector", lambda e, o=cw.ap()[:, i * 16:(i + 1) * 16]:
                      e.scalar_tensor_tensor(out=o, in0=s1.ap(),
                                             scalar=k.t["l0f_w2c"].ap(),
                                             in1=s2.ap(), op0=ALU.mult, op1=ALU.add))
    d_cwb = k.emit("vector", lambda e: e.tensor_copy(out=cwbf.ap(), in_=cw.ap()))
    k.wait("tensor", "dve", d_cwb)
    a_ptev = 0
    for r in range(2):
        if a_ptev:
            k.wait("tensor", "act", a_ptev)
        for j in range(8):
            i = r * 8 + j
            k.emit("tensor", lambda e, src=cwbf.ap()[:, i * 16:(i + 1) * 16],
                   dst=PT.ap()[0:16, j * 128:(j + 1) * 128]:
                   e.transpose(dst, src, k.t["ident_bf"].ap()),
                   inc=("pe", 1) if j == 7 else None)
        k.wait("scalar", "pe", k.cnt["pe"])
        a_ptev = k.emit("scalar", lambda e, rr=r:
                        e.copy(out=cwT.ap()[0:16, rr * 1024:(rr + 1) * 1024],
                               in_=PT.ap()[0:16, :]))

    def load_w1(e_idx, pe_gate):
        k.wait("gpsimd", "pe", pe_gate)
        for c in range(NC):
            k.wait("sync", "pool", k.cnt["pool"])
            k.dma("sync", stage.ap(), aps["l0f_w1"][e_idx][c])
            k.wait("gpsimd", "w", k.cnt["w"])
            k.emit("gpsimd", lambda e, cc=c: e.tensor_copy(
                out=w1bf.ap()[:, cc * 2048:(cc + 1) * 2048], in_=stage.ap()))
        return k.cnt["pool"]

    def load_w2(e_idx, pe_gate):
        k.wait("gpsimd", "pe", pe_gate)
        for cc in range(8):
            k.wait("sync", "pool", k.cnt["pool"])
            k.dma("sync", stage.ap()[:, 0:1024], aps["l0f_w2"][e_idx][2 * cc])
            k.dma("sync", stage.ap()[:, 1024:2048], aps["l0f_w2"][e_idx][2 * cc + 1])
            k.wait("gpsimd", "w", k.cnt["w"])
            k.emit("gpsimd", lambda e, c2=cc: e.tensor_copy(
                out=w2bf.ap()[:, c2 * 2048:(c2 + 1) * 2048], in_=stage.ap()))
        return k.cnt["pool"]

    po_w1 = load_w1(0, 0)
    y_end_pe = 0
    d_epi = d_cwb
    h_evac = {}
    for e_x in range(16):
        po_w2 = load_w2(e_x, y_end_pe)
        k.wait("tensor", "pool", po_w1)
        for blk in range(NB):
            k.wait("scalar", "pe", y_end_pe)
            for m in range(16):
                key = m % 4
                if (blk, e_x) != (0, 0) or m >= 4:
                    k.wait("tensor", "act", h_evac.get(key, 0))
                bank = PL.ap()[:, key * 512: key * 512 + BT]
                for c in range(NC):
                    k.emit("tensor", lambda e, b=bank,
                           l=w1bf.ap()[:, c * 2048 + m * 128: c * 2048 + (m + 1) * 128],
                           r=xnT.ap()[:, c * T + blk * BT: c * T + (blk + 1) * BT],
                           st=(c == 0), sp=(c == NC - 1):
                           e.matmul(b, l, r, start=st, stop=sp),
                           inc=("pe", 1) if c == NC - 1 else None)
                k.wait("scalar", "pe", k.cnt["pe"])
                h_evac[key] = k.emit("scalar", lambda e, b=bank, mm=m, ee=e_x:
                                     e.activation(out=hbf.ap()[:, mm * BT:(mm + 1) * BT],
                                                  in_=b, func=AF.Relu,
                                                  bias=b1fm.ap()[:, ee * 16 + mm:
                                                                 ee * 16 + mm + 1],
                                                  scale=1.0))
            h_end_pe = k.cnt["pe"]
            k.wait("tensor", "act", max(h_evac.values()))
            k.wait("tensor", "pool", po_w2)
            for t in range(2):
                i = blk * 2 + t
                k.wait("tensor", "dve", d_epi)
                for n in range(2):
                    for cp in range(16):
                        k.emit("tensor", lambda e,
                               o=(PA.ap() if n == 0 else PB.ap()),
                               l=hbf.ap()[:, cp * BT + t * 128: cp * BT + (t + 1) * 128],
                               r=w2bf.ap()[:, cp * 1024 + n * 512:
                                           cp * 1024 + (n + 1) * 512],
                               st=(cp == 0), sp=(cp == 15):
                               e.matmul(o, l, r, start=st, stop=sp),
                               inc=("pe", 1) if (n == 1 and cp == 15) else None)
                p_y = k.cnt["pe"]
                k.wait("vector", "pe", p_y)
                k.emit("vector", lambda e, i2=i, ee=e_x,
                       d=R.ap()[:, i * 1024: i * 1024 + 512]:
                       e.scalar_tensor_tensor(out=d, in0=PA.ap(),
                                              scalar=cw.ap()[:, i2 * 16 + ee:
                                                             i2 * 16 + ee + 1],
                                              in1=d, op0=ALU.mult, op1=ALU.add))
                d_epi = k.emit("vector", lambda e, i2=i, ee=e_x,
                               d=R.ap()[:, i * 1024 + 512: (i + 1) * 1024]:
                               e.scalar_tensor_tensor(out=d, in0=PB.ap(),
                                                      scalar=cw.ap()[:, i2 * 16 + ee:
                                                                     i2 * 16 + ee + 1],
                                                      in1=d, op0=ALU.mult, op1=ALU.add))
            y_end_pe = k.cnt["pe"]
        if e_x < 15:
            po_w1 = load_w1(e_x + 1, h_end_pe)
    k.wait("tensor", "pool", po_b2)
    for i in range(NT):
        k.wait("tensor", "dve", d_epi)
        k.emit("tensor", lambda e, l=cwT.ap()[0:16, i * 128:(i + 1) * 128]:
               e.matmul(PA.ap(), l, b2bf.ap()[0:16, 0:512], start=True, stop=True))
        p_b2 = k.emit("tensor", lambda e, l=cwT.ap()[0:16, i * 128:(i + 1) * 128]:
                      e.matmul(PB.ap(), l, b2bf.ap()[0:16, 512:1024],
                               start=True, stop=True), inc=("pe", 1))
        k.wait("vector", "pe", p_b2)
        k.emit("vector", lambda e, d=R.ap()[:, i * 1024: i * 1024 + 512]:
               e.tensor_add(out=d, in0=d, in1=PA.ap()))
        d_epi = k.emit("vector", lambda e, d=R.ap()[:, i * 1024 + 512: (i + 1) * 1024]:
                       e.tensor_add(out=d, in0=d, in1=PB.ap()))
    return d_epi



def alloc_mha(k, px):
    k.arena_reset()
    k.sba(px + "_v", [128, NT * 1024], BF16, at=0)
    k.sba(px + "_yT", [128, NC * T], BF16, at=0)
    k.sba(px + "_y", [128, NT * 1024], BF16, at=32 * 1024)
    k.sba(px + "_wvbf", [128, 8192], BF16, at=64 * 1024)
    k.sba(px + "_wpbf", [128, 8192], BF16, at=64 * 1024)
    k.sba(px + "_q", [64, T], BF16, at=80 * 1024)
    k.sba(px + "_k", [64, T], BF16, at=84 * 1024)
    k.sba(px + "_wqkbf", [128, 1024], BF16, at=88 * 1024)
    k.sb_over(px + "_bqk", [64, 32], F32, "st_var")
    k.sb_over(px + "_bvbf", [1, 1024], BF16, "E")


def emit_mha(k, aps, xn_act, px, qkv_bias, proj_bias,
             scale=0.125, heads=16):
    xnT, R = k.t["xnT"], k.t["R"]
    stage = k.t["wstage"]
    v_sb, y_sb, yT = k.t[px + "_v"], k.t[px + "_y"], k.t[px + "_yT"]
    wvbf, wpbf, wqkbf = k.t[px + "_wvbf"], k.t[px + "_wpbf"], k.t[px + "_wqkbf"]
    q_sb, k_sb = k.t[px + "_q"], k.t[px + "_k"]
    bqk, bvbf = k.t[px + "_bqk"], k.t[px + "_bvbf"]
    E, ET = k.t["E"], k.t["ET"]
    nm, nm2, ssum, rinv = k.t["sm_nm"], k.t["sm_nm2"], k.t["sm_sum"], k.t["sm_rinv"]
    PL, PA, PB, PT, PX = k.t["PL"], k.t["PA"], k.t["PB"], k.t["PT"], k.t["PX"]
    fsc = np.float32(scale)

    k.wait("sync", "act", xn_act)
    k.wait("gpsimd", "act", xn_act)
    for c2 in range(4):
        k.wait("sync", "pool", k.cnt["pool"])
        k.dma("sync", stage.ap()[:, 0:1024], aps[px + "_wv"][2 * c2])
        k.dma("sync", stage.ap()[:, 1024:2048], aps[px + "_wv"][2 * c2 + 1])
        k.wait("gpsimd", "w", k.cnt["w"])
        k.emit("gpsimd", lambda e, cc=c2: e.tensor_copy(
            out=wvbf.ap()[:, cc * 2048:(cc + 1) * 2048], in_=stage.ap()))
    if qkv_bias:
        k.dma("sync", bqk.ap(), aps[px + "_bqk"])
        k.wait("sync", "pool", k.cnt["pool"])
        k.dma("sync", stage.ap()[0:1, 0:1024], aps[px + "_bv"])
        w_sm = k.cnt["w"]
        k.wait("gpsimd", "w", w_sm)
        po_wv = k.emit("gpsimd", lambda e: e.tensor_copy(
            out=bvbf.ap(), in_=stage.ap()[0:1, 0:1024]))
    else:
        w_sm = k.cnt["w"]
        k.wait("gpsimd", "w", w_sm)
        po_wv = k.cnt["pool"]

    k.wait("tensor", "act", xn_act)
    k.wait("tensor", "pool", po_wv)
    k.wait("tensor", "w", w_sm)
    a_v_evac = {}
    for i in range(NT):
        outs = (PA.ap(), PB.ap()) if i % 2 == 0 else (PX.ap(), PL.ap()[:, 0:512])
        if i >= 2:
            k.wait("tensor", "act", a_v_evac[i - 2])
        for c in range(NC):
            for n in range(2):
                k.emit("tensor", lambda e,
                       l=xnT.ap()[:, c * T + i * 128: c * T + (i + 1) * 128],
                       r=wvbf.ap()[:, c * 1024 + n * 512: c * 1024 + (n + 1) * 512],
                       o=outs[n], st=(c == 0):
                       e.matmul(o, l, r, start=st, stop=False))
        if qkv_bias:
            for n in range(2):
                k.emit("tensor", lambda e, o=outs[n],
                       r=bvbf.ap()[0:1, n * 512:(n + 1) * 512]:
                       e.matmul(o, k.t["onesrow_bf"].ap(), r, start=False, stop=True),
                       inc=("pe", 1) if n == 1 else None)
        else:
            for n in range(2):
                k.emit("tensor", lambda e, o=outs[n], nn=n:
                       e.matmul(o, k.t["onesrow_bf"].ap(),
                                k.t["zero_row"].ap()[0:1, 0:512],
                                start=False, stop=True),
                       inc=("pe", 1) if n == 1 else None)
        p_v = k.cnt["pe"]
        k.wait("scalar", "pe", p_v)
        k.emit("scalar", lambda e, s=outs[0], d=v_sb.ap()[:, i * 1024: i * 1024 + 512]:
               e.copy(out=d, in_=s))
        a_v_evac[i] = k.emit("scalar", lambda e, s=outs[1],
                             d=v_sb.ap()[:, i * 1024 + 512: (i + 1) * 1024]:
                             e.copy(out=d, in_=s))
    v_end_pe = k.cnt["pe"]
    a_prev_exp = a_v_evac[NT - 1]

    def load_wp():
        for c2 in range(4):
            k.wait("sync", "pool", k.cnt["pool"])
            k.dma("sync", stage.ap()[:, 0:1024], aps[px + "_wp"][2 * c2])
            k.dma("sync", stage.ap()[:, 1024:2048], aps[px + "_wp"][2 * c2 + 1])
            k.wait("gpsimd", "w", k.cnt["w"])
            if c2 == 0:
                k.wait("gpsimd", "pe", v_end_pe)
            k.emit("gpsimd", lambda e, cc=c2: e.tensor_copy(
                out=wpbf.ap()[:, cc * 2048:(cc + 1) * 2048], in_=stage.ap()))
        return k.cnt["pool"]

    def load_wqk(h, pe_gate):
        k.wait("sync", "pool", k.cnt["pool"])
        k.dma("sync", wqkbf_stage_ap(), aps[px + "_wqk"][h].rearrange("c p j -> p c j"))
        k.wait("gpsimd", "w", k.cnt["w"])
        k.wait("gpsimd", "pe", pe_gate)
        return k.emit("gpsimd", lambda e: e.tensor_copy(out=wqkbf.ap(),
                                                        in_=stage.ap()[:, 0:1024]))

    def wqkbf_stage_ap():
        return stage.ap()[:, 0:1024].rearrange("p (c j) -> p c j", c=NC)

    po_wqk = load_wqk(0, 0)
    d_prev_radd = 0
    a_prev_ptevac = 0
    d_yevac_gate = 0
    qk_end = {}
    for h in range(heads):
        k.wait("tensor", "pool", po_wqk)
        for (col, dst, bcol) in ((0, q_sb, 2 * h), (64, k_sb, 2 * h + 1)):
            k.wait("tensor", "act", a_prev_exp)
            for c in range(NC):
                for t4 in range(4):
                    k.emit("tensor", lambda e,
                           l=wqkbf.ap()[:, c * 128 + col: c * 128 + col + 64],
                           r=xnT.ap()[:, c * T + t4 * 512: c * T + (t4 + 1) * 512],
                           o=PL.ap()[0:64, t4 * 512:(t4 + 1) * 512],
                           st=(c == 0), sp=(c == NC - 1):
                           e.matmul(o, l, r, start=st, stop=sp),
                           inc=("pe", 1) if (c == NC - 1 and t4 == 3) else None)
            p_qk = k.cnt["pe"]
            k.wait("scalar", "pe", p_qk)
            if qkv_bias:
                a_prev_exp = k.emit("scalar", lambda e, d=dst, bc=bcol:
                                    e.activation(out=d.ap(), in_=PL.ap()[0:64, :],
                                                 func=AF.Identity,
                                                 bias=bqk.ap()[:, bc:bc + 1],
                                                 scale=1.0))
            else:
                a_prev_exp = k.emit("scalar", lambda e, d=dst:
                                    e.copy(out=d.ap(), in_=PL.ap()[0:64, :]))
        qk_end[h] = k.cnt["pe"]
        if h + 1 < heads:
            po_wqk = load_wqk(h + 1, qk_end[h])
        elif h + 1 == heads:
            po_wp = load_wp()
        for i in range(NT):
            L = (i + 1) * 128
            nkc = (L + 511) // 512
            k.wait("tensor", "act", a_prev_exp)
            for kc in range(nkc):
                w_ = min(512, L - kc * 512)
                k.emit("tensor", lambda e,
                       l=q_sb.ap()[0:64, i * 128:(i + 1) * 128],
                       r=k_sb.ap()[0:64, kc * 512: kc * 512 + w_],
                       o=PL.ap()[:, kc * 512: kc * 512 + w_]:
                       e.matmul(o, l, r, start=True, stop=True),
                       inc=("pe", 1) if kc == nkc - 1 else None)
            p_log = k.cnt["pe"]
            k.wait("vector", "pe", p_log)
            k.emit("vector", lambda e, o=PL.ap()[:, i * 128:(i + 1) * 128]:
                   e.tensor_add(out=o, in0=o, in1=k.t["causal"].ap()))
            k.emit("vector", lambda e, o=PL.ap()[:, 0:L]:
                   e.reduce_max(out=nm.ap(), in_=o, axis=AX.X, negate=True))
            d_max = k.emit("vector", lambda e: e.tensor_scalar_mul(
                out=nm2.ap(), in0=nm.ap(), scalar1=float(scale)))
            k.wait("scalar", "dve", d_max)
            a_exp = k.emit("scalar", lambda e, o=E.ap()[:, 0:L], i_=PL.ap()[:, 0:L]:
                           e.activation(out=o, in_=i_, func=AF.Exp,
                                        bias=nm2.ap(), scale=fsc,
                                        accum_out=ssum.ap()))
            a_prev_exp = a_exp
            k.wait("vector", "act", a_exp)
            k.emit("vector", lambda e: e.reciprocal(out=rinv.ap(), in_=ssum.ap()))
            d_esc = k.emit("vector", lambda e, o=E.ap()[:, 0:L]:
                           e.tensor_scalar_mul(out=o, in0=o, scalar1=rinv.ap()))
            k.wait("tensor", "dve", d_esc)
            for j0 in range(0, i + 1, 8):
                jn = min(8, i + 1 - j0)
                k.wait("tensor", "act", a_prev_ptevac)
                for j in range(jn):
                    k.emit("tensor", lambda e,
                           src=E.ap()[:, (j0 + j) * 128:(j0 + j + 1) * 128],
                           dst=PT.ap()[:, j * 128:(j + 1) * 128]:
                           e.transpose(dst, src, k.t["ident_bf"].ap()),
                           inc=("pe", 1) if j == jn - 1 else None)
                k.wait("scalar", "pe", k.cnt["pe"])
                a_prev_ptevac = k.emit("scalar", lambda e, o=ET.ap()[:, j0 * 128:
                                                                    (j0 + jn) * 128],
                                       s=PT.ap()[:, 0:jn * 128]:
                                       e.copy(out=o, in_=s))
            bank = PA if (h * NT + i) % 2 == 0 else PB
            k.wait("tensor", "act", max(a_prev_ptevac, d_yevac_gate))
            k.wait("tensor", "act", a_prev_ptevac)
            for j in range(i + 1):
                k.emit("tensor", lambda e,
                       l=ET.ap()[:, j * 128:(j + 1) * 128],
                       r=v_sb.ap()[:, j * 1024 + h * 64: j * 1024 + h * 64 + 64],
                       o=bank.ap()[:, 0:64], st=(j == 0), sp=(j == i):
                       e.matmul(o, l, r, start=st, stop=sp),
                       inc=("pe", 1) if j == i else None)
            p_av = k.cnt["pe"]
            k.wait("scalar", "pe", p_av)
            d_yevac_gate = k.emit("scalar", lambda e, s=bank.ap()[:, 0:64],
                                  d=y_sb.ap()[:, i * 1024 + h * 64:
                                              i * 1024 + h * 64 + 64]:
                                  e.copy(out=d, in_=s))
    a_yev = k.cnt["act"]
    for i in range(NT):
        k.wait("tensor", "act", a_yev if i == 0 else a_prev_ptevac)
        for c in range(NC):
            k.emit("tensor", lambda e, src=y_sb.ap()[:, i * 1024 + c * 128:
                                                     i * 1024 + (c + 1) * 128],
                   dst=PT.ap()[:, c * 128:(c + 1) * 128]:
                   e.transpose(dst, src, k.t["ident_bf"].ap()),
                   inc=("pe", 1) if c == NC - 1 else None)
        k.wait("scalar", "pe", k.cnt["pe"])
        dst = yT.ap().rearrange("p (c t) -> p c t", c=NC)[:, :, i * 128:(i + 1) * 128]
        src = PT.ap().rearrange("p (c t) -> p c t", c=NC)
        a_prev_ptevac = k.emit("scalar", lambda e, s=src, d=dst: e.copy(out=d, in_=s))
    a_yT = a_prev_ptevac
    if proj_bias:
        k.wait("sync", "pool", k.cnt["pool"])
        k.dma("sync", stage.ap()[0:1, 0:1024], aps[px + "_bp"])
        k.wait("gpsimd", "w", k.cnt["w"])
        k.wait("gpsimd", "pe", k.cnt["pe"])
        po_bp = k.emit("gpsimd", lambda e: e.tensor_copy(
            out=bvbf.ap(), in_=stage.ap()[0:1, 0:1024]))
    else:
        po_bp = k.cnt["pool"]
    k.wait("tensor", "act", a_yT)
    k.wait("tensor", "pool", po_bp)
    d_epi = 0
    for i in range(NT):
        k.wait("tensor", "dve", d_epi)
        for n in range(2):
            for c in range(NC):
                k.emit("tensor", lambda e,
                       l=yT.ap()[:, c * T + i * 128: c * T + (i + 1) * 128],
                       r=wpbf.ap()[:, c * 1024 + n * 512: c * 1024 + (n + 1) * 512],
                       o=(PA.ap() if n == 0 else PB.ap()), st=(c == 0):
                       e.matmul(o, l, r, start=st, stop=False))
        if proj_bias:
            for n in range(2):
                k.emit("tensor", lambda e, o=(PA.ap() if n == 0 else PB.ap()),
                       r=bvbf.ap()[0:1, n * 512:(n + 1) * 512]:
                       e.matmul(o, k.t["onesrow_bf"].ap(), r, start=False, stop=True),
                       inc=("pe", 1) if n == 1 else None)
        else:
            for n in range(2):
                k.emit("tensor", lambda e, o=(PA.ap() if n == 0 else PB.ap()), nn=n:
                       e.matmul(o, k.t["onesrow_bf"].ap(),
                                k.t["zero_row"].ap()[0:1, 0:512],
                                start=False, stop=True),
                       inc=("pe", 1) if n == 1 else None)
        p_pj = k.cnt["pe"]
        k.wait("vector", "pe", p_pj)
        k.emit("vector", lambda e, d=R.ap()[:, i * 1024: i * 1024 + 512]:
               e.tensor_add(out=d, in0=d, in1=PA.ap()))
        d_epi = k.emit("vector", lambda e,
                       d=R.ap()[:, i * 1024 + 512: (i + 1) * 1024]:
                       e.tensor_add(out=d, in0=d, in1=PB.ap()))
    return d_epi



def alloc_l1_ffn(k):
    k.arena_reset()
    k.sba("l1f_kact", [128, NC * T], BF16, at=0)
    k.sba("l1f_wkmu", [128, 8192], BF16, at=32 * 1024)
    k.sba("l1f_wkm1", [128, 8192], BF16, at=48 * 1024)
    k.sba("l1f_wrmu", [128, 8192], BF16, at=32 * 1024)
    k.sba("l1f_wrm1", [128, 8192], BF16, at=48 * 1024)
    k.sba("l1f_wvbf", [128, 8192], BF16, at=64 * 1024)
    k.sba("l1f_xs0", [128, 4096], BF16, at=80 * 1024)
    k.sb_over("l1f_rt", [128, 1024], F32, "E")
    k.sba("l1f_mur", [128, 8], F32, at=88 * 1024)
    k.sba("l1f_murm", [128, 8], F32, at=88 * 1024 + 64)
    k.sba("l1f_muk", [128, 8], F32, at=88 * 1024 + 128)
    k.sba("l1f_mukm", [128, 8], F32, at=88 * 1024 + 192)


def emit_l1_ffn(k, aps, xn_act):
    xnT, R = k.t["xnT"], k.t["R"]
    stage = k.t["wstage"]
    kact = k.t["l1f_kact"]
    wkmu, wkm1 = k.t["l1f_wkmu"], k.t["l1f_wkm1"]
    wrmu, wrm1 = k.t["l1f_wrmu"], k.t["l1f_wrm1"]
    wvbf, xs0, rt = k.t["l1f_wvbf"], k.t["l1f_xs0"], k.t["l1f_rt"]
    mur, murm = k.t["l1f_mur"], k.t["l1f_murm"]
    muk, mukm = k.t["l1f_muk"], k.t["l1f_mukm"]
    PL, PA, PB = k.t["PL"], k.t["PA"], k.t["PB"]

    def xsh(c, start, width):
        if start == 0:
            return xs0.ap()[:, c * 512: c * 512 + width]
        return xnT.ap()[:, c * T + start - 1: c * T + start - 1 + width]

    k.wait("sync", "act", xn_act)
    k.wait("gpsimd", "act", xn_act)
    k.dma("sync", mur.ap(), aps["l1f_mur"])
    k.dma("sync", muk.ap(), aps["l1f_muk"])
    w_mu = k.cnt["w"]
    k.wait("vector", "w", w_mu)
    k.wait("vector", "act", xn_act)
    k.emit("vector", lambda e: e.tensor_scalar(out=murm.ap(), in0=mur.ap(),
                                               scalar1=-1.0, scalar2=1.0,
                                               op0=ALU.mult, op1=ALU.add))
    k.emit("vector", lambda e: e.tensor_scalar(out=mukm.ap(), in0=muk.ap(),
                                               scalar1=-1.0, scalar2=1.0,
                                               op0=ALU.mult, op1=ALU.add))
    for c in range(NC):
        k.emit("vector", lambda e, cc=c: e.memset(
            xs0.ap()[:, cc * 512: cc * 512 + 1], 0))
        k.emit("vector", lambda e, cc=c: e.tensor_copy(
            out=xs0.ap()[:, cc * 512 + 1: (cc + 1) * 512],
            in_=xnT.ap()[:, cc * T: cc * T + 511]))
    d_xs0 = k.cnt["dve"]

    def load_scaled(src, dmu, dm1, mu_t, m1_t, pe_gate):
        for c in range(NC):
            k.wait("sync", "pool", k.cnt["pool"])
            k.dma("sync", stage.ap()[:, 0:1024], aps[src][c])
            k.wait("gpsimd", "w", k.cnt["w"])
            if c == 0:
                k.wait("gpsimd", "pe", pe_gate)
                k.wait("gpsimd", "dve", d_xs0)
            k.emit("gpsimd", lambda e, cc=c, d_=dmu, m_=mu_t: e.tensor_scalar_mul(
                out=d_.ap()[:, cc * 1024:(cc + 1) * 1024],
                in0=stage.ap()[:, 0:1024], scalar1=m_.ap()[:, cc:cc + 1]))
            k.emit("gpsimd", lambda e, cc=c, d_=dm1, m_=m1_t: e.tensor_scalar_mul(
                out=d_.ap()[:, cc * 1024:(cc + 1) * 1024],
                in0=stage.ap()[:, 0:1024], scalar1=m_.ap()[:, cc:cc + 1]))
        return k.cnt["pool"]

    po_wk = load_scaled("l1f_wk", wkmu, wkm1, muk, mukm, 0)

    k.wait("tensor", "pool", po_wk)
    k.wait("tensor", "act", xn_act)
    k.wait("tensor", "dve", d_xs0)
    bank_gate = {}
    a_relu_gate = {}
    for m2 in range(8):
        for blk in range(4):
            key = (m2 * 4 + blk) % 4
            bank = PL.ap()[:, key * 512:(key + 1) * 512]
            k.wait("tensor", "dve", bank_gate.get(key, 0))
            for c in range(NC):
                k.emit("tensor", lambda e, b=bank,
                       l=wkmu.ap()[:, c * 1024 + m2 * 128: c * 1024 + (m2 + 1) * 128],
                       r=xnT.ap()[:, c * T + blk * 512: c * T + (blk + 1) * 512],
                       st=(c == 0):
                       e.matmul(b, l, r, start=st, stop=False))
            for c in range(NC):
                k.emit("tensor", lambda e, b=bank,
                       l=wkm1.ap()[:, c * 1024 + m2 * 128: c * 1024 + (m2 + 1) * 128],
                       r=xsh(c, blk * 512, 512), sp=(c == NC - 1):
                       e.matmul(b, l, r, start=False, stop=sp),
                       inc=("pe", 1) if c == NC - 1 else None)
            p_k = k.cnt["pe"]
            k.wait("scalar", "pe", p_k)
            k.wait("scalar", "dve", max(bank_gate.values(), default=0))
            a_relu = k.emit("scalar", lambda e, b=bank: e.activation(
                out=rt.ap()[:, 0:512], in_=b, func=AF.Relu))
            k.wait("vector", "act", a_relu)
            bank_gate[key] = k.emit("vector", lambda e, b=bank,
                                    o=kact.ap()[:, m2 * T + blk * 512:
                                                m2 * T + (blk + 1) * 512]:
                                    e.tensor_mul(out=o, in0=rt.ap()[:, 0:512], in1=b))
    k_end_pe = k.cnt["pe"]
    d_kact = k.cnt["dve"]

    po_wr = load_scaled("l1f_wr", wrmu, wrm1, mur, murm, k_end_pe)
    for c2 in range(4):
        k.wait("sync", "pool", k.cnt["pool"])
        k.dma("sync", stage.ap()[:, 0:1024], aps["l1f_wv"][2 * c2])
        k.dma("sync", stage.ap()[:, 1024:2048], aps["l1f_wv"][2 * c2 + 1])
        k.wait("gpsimd", "w", k.cnt["w"])
        k.emit("gpsimd", lambda e, cc=c2: e.tensor_copy(
            out=wvbf.ap()[:, cc * 2048:(cc + 1) * 2048], in_=stage.ap()))
    po_wv = k.cnt["pool"]

    k.wait("tensor", "pool", po_wv)
    k.wait("tensor", "dve", d_kact)
    a_sig = 0
    d_epi = 0
    for i in range(NT):
        k.wait("tensor", "act", a_sig)
        k.wait("tensor", "dve", d_epi)
        for n in range(2):
            for c in range(NC):
                k.emit("tensor", lambda e,
                       l=xnT.ap()[:, c * T + i * 128: c * T + (i + 1) * 128],
                       r=wrmu.ap()[:, c * 1024 + n * 512: c * 1024 + (n + 1) * 512],
                       o=PL.ap()[:, n * 512:(n + 1) * 512], st=(c == 0):
                       e.matmul(o, l, r, start=st, stop=False))
            for c in range(NC):
                k.emit("tensor", lambda e,
                       l=xsh(c, i * 128, 128),
                       r=wrm1.ap()[:, c * 1024 + n * 512: c * 1024 + (n + 1) * 512],
                       o=PL.ap()[:, n * 512:(n + 1) * 512], sp=(c == NC - 1):
                       e.matmul(o, l, r, start=False, stop=sp),
                       inc=("pe", 1) if (n == 1 and c == NC - 1) else None)
        p_r = k.cnt["pe"]
        k.wait("scalar", "pe", p_r)
        k.wait("scalar", "dve", d_epi)
        k.emit("scalar", lambda e: e.activation(out=rt.ap()[:, 0:512],
                                                in_=PL.ap()[:, 0:512],
                                                func=AF.Sigmoid))
        a_sig = k.emit("scalar", lambda e: e.activation(
            out=rt.ap()[:, 512:1024], in_=PL.ap()[:, 512:1024], func=AF.Sigmoid))
        k.wait("tensor", "act", a_sig)
        for n in range(2):
            for m2 in range(NC):
                k.emit("tensor", lambda e,
                       l=kact.ap()[:, m2 * T + i * 128: m2 * T + (i + 1) * 128],
                       r=wvbf.ap()[:, m2 * 1024 + n * 512: m2 * 1024 + (n + 1) * 512],
                       o=(PA.ap() if n == 0 else PB.ap()),
                       st=(m2 == 0), sp=(m2 == NC - 1):
                       e.matmul(o, l, r, start=st, stop=sp),
                       inc=("pe", 1) if (n == 1 and m2 == NC - 1) else None)
        p_v = k.cnt["pe"]
        k.wait("vector", "pe", p_v)
        for n in range(2):
            k.emit("vector", lambda e, n2=n, ps=(PA.ap() if n == 0 else PB.ap()):
                   e.tensor_mul(out=rt.ap()[:, n2 * 512:(n2 + 1) * 512],
                                in0=rt.ap()[:, n2 * 512:(n2 + 1) * 512], in1=ps))
            d_epi = k.emit("vector", lambda e, n2=n,
                           d=R.ap()[:, i * 1024 + n * 512:
                                    i * 1024 + (n + 1) * 512]:
                           e.tensor_add(out=d, in0=d,
                                        in1=rt.ap()[:, n2 * 512:(n2 + 1) * 512]))
    return d_epi



def alloc_l2_ffn(k):
    k.arena_reset()
    k.sba("l2f_wdbf", [128, 24576], BF16, at=0)
    k.sba("l2f_h", [128, 24 * 512], BF16, at=48 * 1024)
    k.sba("l2f_sg", [128, 1024], F32, at=72 * 1024)
    k.sba("l2f_wgu0", [128, 2048], BF16, at=76 * 1024)
    k.sba("l2f_wgu1", [128, 2048], BF16, at=80 * 1024)


def emit_l2_ffn(k, aps, xn_act):
    xnT, R = k.t["xnT"], k.t["R"]
    stage = k.t["wstage"]
    wdbf, hbf, sg = k.t["l2f_wdbf"], k.t["l2f_h"], k.t["l2f_sg"]
    wgu = (k.t["l2f_wgu0"], k.t["l2f_wgu1"])
    PL, PA, PB = k.t["PL"], k.t["PA"], k.t["PB"]
    SC = np.float32(1.0 / 127.0)

    k.wait("sync", "act", xn_act)
    k.wait("gpsimd", "act", xn_act)
    for r in range(12):
        k.wait("sync", "pool", k.cnt["pool"])
        k.dma("sync", stage.ap().bitcast(I32)[:, 0:1024], aps["l2f_wd"][2 * r])
        k.dma("sync", stage.ap().bitcast(I32)[:, 1024:2048], aps["l2f_wd"][2 * r + 1])
        k.wait("gpsimd", "w", k.cnt["w"])
        k.emit("gpsimd", lambda e, rr=r: e.tensor_copy(
            out=wdbf.ap()[:, rr * 2048:(rr + 1) * 2048],
            in_=stage.ap().bitcast(I32)))
    po_wd = k.cnt["pool"]

    k.wait("tensor", "act", xn_act)
    d_epi = 0
    bank_gate = {}
    wgu_user_pe = {}
    po_wgu = {}

    for blk in range(4):
        for m in range(24):
            idx = blk * 24 + m
            buf = idx % 2
            k.wait("sync", "pool", k.cnt["pool"])
            k.dma("sync",
                  stage.ap().bitcast(I32).rearrange("p (c j) -> p c j", c=NC),
                  aps["l2f_wgu"][m].rearrange("c p j -> p c j"))
            k.wait("gpsimd", "w", k.cnt["w"])
            if idx >= 2:
                k.wait("gpsimd", "pe", wgu_user_pe[idx - 2])
            po_wgu[idx] = k.emit("gpsimd", lambda e, b=buf: e.tensor_copy(
                out=wgu[b].ap(), in_=stage.ap().bitcast(I32)))
            b0 = (m % 2) * 2
            gbank = PL.ap()[:, b0 * 512:(b0 + 1) * 512]
            ubank = PL.ap()[:, (b0 + 1) * 512:(b0 + 2) * 512]
            k.wait("tensor", "pool", po_wgu[idx])
            k.wait("tensor", "dve", bank_gate.get(m % 2, 0))
            for c in range(NC):
                k.emit("tensor", lambda e, o=gbank, b=buf,
                       l=wgu[buf].ap()[:, c * 256: c * 256 + 128],
                       r=xnT.ap()[:, c * T + blk * 512: c * T + (blk + 1) * 512],
                       st=(c == 0), sp=(c == NC - 1):
                       e.matmul(o, l, r, start=st, stop=sp))
                k.emit("tensor", lambda e, o=ubank, b=buf,
                       l=wgu[buf].ap()[:, c * 256 + 128: c * 256 + 256],
                       r=xnT.ap()[:, c * T + blk * 512: c * T + (blk + 1) * 512],
                       st=(c == 0), sp=(c == NC - 1):
                       e.matmul(o, l, r, start=st, stop=sp),
                       inc=("pe", 1) if c == NC - 1 else None)
            wgu_user_pe[idx] = k.cnt["pe"]
            p_gu = k.cnt["pe"]
            k.wait("scalar", "pe", p_gu)
            k.wait("scalar", "dve", max(bank_gate.values(), default=0))
            a_silu = k.emit("scalar", lambda e, b=gbank: e.activation(
                out=sg.ap()[:, 0:512], in_=b, func=AF.Silu, scale=SC))
            k.wait("vector", "act", a_silu)
            bank_gate[m % 2] = k.emit("vector", lambda e, b=ubank, mm=m:
                                      e.scalar_tensor_tensor(
                                          out=hbf.ap()[:, mm * 512:(mm + 1) * 512],
                                          in0=b, scalar=1.0 / 127.0,
                                          in1=sg.ap()[:, 0:512],
                                          op0=ALU.mult, op1=ALU.mult))
        k.wait("tensor", "dve", max(bank_gate.values()))
        k.wait("tensor", "pool", po_wd)
        for t in range(4):
            i = blk * 4 + t
            k.wait("tensor", "dve", d_epi)
            for n in range(2):
                for cp in range(24):
                    k.emit("tensor", lambda e,
                           o=(PA.ap() if n == 0 else PB.ap()),
                           l=hbf.ap()[:, cp * 512 + t * 128: cp * 512 + (t + 1) * 128],
                           r=wdbf.ap()[:, cp * 1024 + n * 512:
                                       cp * 1024 + (n + 1) * 512],
                           st=(cp == 0), sp=(cp == 23):
                           e.matmul(o, l, r, start=st, stop=sp),
                           inc=("pe", 1) if (n == 1 and cp == 23) else None)
            p_dn = k.cnt["pe"]
            k.wait("vector", "pe", p_dn)
            k.emit("vector", lambda e, d=R.ap()[:, i * 1024: i * 1024 + 512]:
                   e.scalar_tensor_tensor(out=d, in0=PA.ap(), scalar=1.0 / 127.0,
                                          in1=d, op0=ALU.mult, op1=ALU.add))
            d_epi = k.emit("vector", lambda e,
                           d=R.ap()[:, i * 1024 + 512: (i + 1) * 1024]:
                           e.scalar_tensor_tensor(out=d, in0=PB.ap(),
                                                  scalar=1.0 / 127.0,
                                                  in1=d, op0=ALU.mult, op1=ALU.add))
    return d_epi



def alloc_l3_attn(k):
    k.arena_reset()
    k.sba("l3_v", [128, NT * 1024], BF16, at=0)
    k.sba("l3_wvbf", [128, 8192], BF16, at=32 * 1024)
    k.sba("l3_wqkbf", [128, 8192], BF16, at=32 * 1024)
    k.sba("l3_qk", [64, T], BF16, at=48 * 1024)
    k.sba("l3_wb", [128, T], F32, at=52 * 1024)
    k.sba("l3_sqrow", [1, T], F32, at=52 * 1024)
    k.sba("l3_sqsq", [64, T], F32, at=60 * 1024)
    k.sba("l3_mjb", [128, T], F32, at=60 * 1024)
    k.sba("l3_sqjb", [128, T], F32, at=68 * 1024)
    k.sba("l3_massT", [16, T], BF16, at=76 * 1024)
    k.sba("l3_mass", [128, 256], F32, at=80 * 1024)
    k.sba("l3_sqT", [128, 16], F32, at=81 * 1024)
    k.sba("l3_sqTe", [128, 16], F32, at=81 * 1024 + 64)
    k.sba("l3_oneh", [16, 128], BF16, at=81 * 1024 + 128)
    k.sba("l3_bqk", [64, 16], F32, at=81 * 1024 + 384)
    k.sba("l3_wmbf", [128, 128], BF16, at=81 * 1024 + 448)
    k.sba("l3_ones64", [64, 1], F32, at=81 * 1024 + 704)
    k.sba("l3_bmbf", [1, 16], BF16, at=81 * 1024 + 768)
    k.sb_over("l3_bvbf", [1, 1024], BF16, "xn_tm")


def emit_l3_attn(k, aps, xn_act):
    xnT, R = k.t["xnT"], k.t["R"]
    stage = k.t["wstage"]
    v_sb, wvbf, wqkbf = k.t["l3_v"], k.t["l3_wvbf"], k.t["l3_wqkbf"]
    qk_sb, wb, sqsq = k.t["l3_qk"], k.t["l3_wb"], k.t["l3_sqsq"]
    mjb, sqjb, sqrow = k.t["l3_mjb"], k.t["l3_sqjb"], k.t["l3_sqrow"]
    massT, mass = k.t["l3_massT"], k.t["l3_mass"]
    sqT, sqTe = k.t["l3_sqT"], k.t["l3_sqTe"]
    oneh, bqk, wmbf = k.t["l3_oneh"], k.t["l3_bqk"], k.t["l3_wmbf"]
    ones64, bvbf, bmbf = k.t["l3_ones64"], k.t["l3_bvbf"], k.t["l3_bmbf"]
    E, ET = k.t["E"], k.t["ET"]
    nm, nm2, ssum, rinv = k.t["sm_nm"], k.t["sm_nm2"], k.t["sm_sum"], k.t["sm_rinv"]
    PL, PA, PB, PT, PX = k.t["PL"], k.t["PA"], k.t["PB"], k.t["PT"], k.t["PX"]
    one = k.t["c_one"]

    k.wait("sync", "act", xn_act)
    k.wait("gpsimd", "act", xn_act)
    for c2 in range(4):
        k.wait("sync", "pool", k.cnt["pool"])
        k.dma("sync", stage.ap()[:, 0:1024], aps["l3_wv"][2 * c2])
        k.dma("sync", stage.ap()[:, 1024:2048], aps["l3_wv"][2 * c2 + 1])
        k.wait("gpsimd", "w", k.cnt["w"])
        k.emit("gpsimd", lambda e, cc=c2: e.tensor_copy(
            out=wvbf.ap()[:, cc * 2048:(cc + 1) * 2048], in_=stage.ap()))
    k.dma("sync", bqk.ap(), aps["l3_bqk"])
    k.wait("sync", "pool", k.cnt["pool"])
    k.dma("sync", stage.ap()[:, 0:128], aps["l3_wmass_flat"])
    k.dma("sync", stage.ap()[0:1, 1024:2048], aps["l3_bv"])
    k.dma("sync", stage.ap()[0:1, 128:144], aps["l3_bm"])
    w_sm = k.cnt["w"]
    k.wait("gpsimd", "w", w_sm)
    k.emit("gpsimd", lambda e: e.tensor_copy(out=wmbf.ap(),
                                             in_=stage.ap()[:, 0:128]))
    k.emit("gpsimd", lambda e: e.tensor_copy(out=bvbf.ap(),
                                             in_=stage.ap()[0:1, 1024:2048]))
    po_bv = k.emit("gpsimd", lambda e: e.tensor_copy(
        out=bmbf.ap(), in_=stage.ap()[0:1, 128:144]))
    k.wait("vector", "w", w_sm)
    d_o64 = k.emit("vector", lambda e: e.memset(ones64.ap(), 1.0))

    k.wait("tensor", "act", xn_act)
    k.wait("tensor", "pool", po_bv)
    a_v_evac = {}
    for i in range(NT):
        outs = (PA.ap(), PB.ap()) if i % 2 == 0 else (PX.ap(), PL.ap()[:, 0:512])
        if i >= 2:
            k.wait("tensor", "act", a_v_evac[i - 2])
        for c in range(NC):
            for n in range(2):
                k.emit("tensor", lambda e,
                       l=xnT.ap()[:, c * T + i * 128: c * T + (i + 1) * 128],
                       r=wvbf.ap()[:, c * 1024 + n * 512: c * 1024 + (n + 1) * 512],
                       o=outs[n], st=(c == 0):
                       e.matmul(o, l, r, start=st, stop=False))
        for n in range(2):
            k.emit("tensor", lambda e, o=outs[n],
                   r=bvbf.ap()[0:1, n * 512:(n + 1) * 512]:
                   e.matmul(o, k.t["onesrow_bf"].ap(), r, start=False, stop=True),
                   inc=("pe", 1) if n == 1 else None)
        p_v = k.cnt["pe"]
        k.wait("scalar", "pe", p_v)
        k.emit("scalar", lambda e, s=outs[0], d=v_sb.ap()[:, i * 1024: i * 1024 + 512]:
               e.copy(out=d, in_=s))
        a_v_evac[i] = k.emit("scalar", lambda e, s=outs[1],
                             d=v_sb.ap()[:, i * 1024 + 512: (i + 1) * 1024]:
                             e.copy(out=d, in_=s))
    v_end_pe = k.cnt["pe"]
    a_px_free = a_v_evac[NT - 1]

    d_mass = 0
    for i in range(NT):
        if d_mass:
            k.wait("tensor", "dve", d_mass)
        else:
            k.wait("tensor", "act", a_px_free)
        for c in range(NC):
            k.emit("tensor", lambda e,
                   l=xnT.ap()[:, c * T + i * 128: c * T + (i + 1) * 128],
                   r=wmbf.ap()[:, c * 16:(c + 1) * 16],
                   o=PX.ap()[:, 0:16], st=(c == 0):
                   e.matmul(o, l, r, start=st, stop=False))
        k.emit("tensor", lambda e: e.matmul(PX.ap()[:, 0:16],
                                            k.t["onesrow_bf"].ap(),
                                            bmbf.ap()[0:1, 0:16],
                                            start=False, stop=True),
               inc=("pe", 1))
        p_m = k.cnt["pe"]
        k.wait("vector", "pe", p_m)
        d_cp = k.emit("vector", lambda e: e.tensor_copy(out=k.t["st_sum"].ap(),
                                                        in_=PX.ap()[:, 0:16]))
        k.wait("scalar", "dve", d_cp)
        k.emit("scalar", lambda e: e.activation(out=k.t["st_sq"].ap(),
                                                in_=k.t["st_sum"].ap(), func=AF.Abs))
        k.emit("scalar", lambda e: e.activation(out=k.t["st_sq"].ap(),
                                                in_=k.t["st_sq"].ap(), func=AF.Exp,
                                                scale=np.float32(-1.0)))
        a_sp = k.emit("scalar", lambda e: e.activation(out=k.t["st_sq"].ap(),
                                                       in_=k.t["st_sq"].ap(),
                                                       func=AF.Ln, bias=one.ap(),
                                                       scale=1.0))
        k.wait("vector", "act", a_sp)
        k.emit("vector", lambda e: e.tensor_scalar_max(out=k.t["st_mean"].ap(),
                                                       in0=k.t["st_sum"].ap(),
                                                       scalar1=0.0))
        d_mass = k.emit("vector", lambda e, o=mass.ap()[:, i * 16:(i + 1) * 16]:
                        e.tensor_add(out=o, in0=k.t["st_mean"].ap(),
                                     in1=k.t["st_sq"].ap()))
    d_mbf = k.emit("vector", lambda e: e.tensor_copy(out=k.t["xn_tm"].ap()[:, 0:256],
                                                     in_=mass.ap()))
    k.wait("tensor", "dve", d_mbf)
    a_ptev = 0
    for r in range(2):
        if a_ptev:
            k.wait("tensor", "act", a_ptev)
        for j in range(8):
            i = r * 8 + j
            k.emit("tensor", lambda e, src=k.t["xn_tm"].ap()[:, i * 16:(i + 1) * 16],
                   dst=PT.ap()[0:16, j * 128:(j + 1) * 128]:
                   e.transpose(dst, src, k.t["ident_bf"].ap()),
                   inc=("pe", 1) if j == 7 else None)
        k.wait("scalar", "pe", k.cnt["pe"])
        a_ptev = k.emit("scalar", lambda e, rr=r:
                        e.copy(out=massT.ap()[0:16, rr * 1024:(rr + 1) * 1024],
                               in_=PT.ap()[0:16, :]))

    for c2 in range(4):
        k.wait("sync", "pool", k.cnt["pool"])
        k.dma("sync", stage.ap()[:, 0:1024], aps["l3_wqk"][2 * c2])
        k.dma("sync", stage.ap()[:, 1024:2048], aps["l3_wqk"][2 * c2 + 1])
        k.wait("gpsimd", "w", k.cnt["w"])
        if c2 == 0:
            k.wait("gpsimd", "pe", v_end_pe)
        k.emit("gpsimd", lambda e, cc=c2: e.tensor_copy(
            out=wqkbf.ap()[:, cc * 2048:(cc + 1) * 2048], in_=stage.ap()))
    po_wqk = k.cnt["pool"]

    a_prev_exp = a_ptev
    a_prev_ptevac = a_ptev
    d_radd = k.cnt["dve"]
    d_pl_read = k.cnt["dve"]
    oneh_pe = 0
    k.wait("tensor", "dve", d_mbf)
    for h in range(16):
        k.wait("tensor", "pool", po_wqk)
        k.wait("tensor", "act", a_prev_exp)
        for c in range(NC):
            for t4 in range(4):
                k.emit("tensor", lambda e,
                       l=wqkbf.ap()[:, c * 1024 + h * 64: c * 1024 + h * 64 + 64],
                       r=xnT.ap()[:, c * T + t4 * 512: c * T + (t4 + 1) * 512],
                       o=PL.ap()[0:64, t4 * 512:(t4 + 1) * 512],
                       st=(c == 0), sp=(c == NC - 1):
                       e.matmul(o, l, r, start=st, stop=sp),
                       inc=("pe", 1) if (c == NC - 1 and t4 == 3) else None)
        p_qk = k.cnt["pe"]
        k.wait("scalar", "pe", p_qk)
        a_qk = k.emit("scalar", lambda e, bc=h:
                      e.activation(out=qk_sb.ap(), in_=PL.ap()[0:64, :],
                                   func=AF.Identity, bias=bqk.ap()[:, bc:bc + 1],
                                   scale=1.0))
        a_sq = k.emit("scalar", lambda e: e.activation(out=sqsq.ap(), in_=qk_sb.ap(),
                                                       func=AF.Square))
        k.wait("tensor", "act", a_sq)
        k.wait("tensor", "dve", d_o64)
        a_evr = k.cnt["act"]
        for t4 in range(4):
            if t4 > 0:
                k.wait("tensor", "act", a_evr)
            k.emit("tensor", lambda e, tt=t4:
                   e.matmul(PX.ap()[0:1, 0:512], ones64.ap(),
                            sqsq.ap()[0:64, tt * 512:(tt + 1) * 512],
                            start=True, stop=True), inc=("pe", 1))
            k.wait("scalar", "pe", k.cnt["pe"])
            a_evr = k.emit("scalar", lambda e, tt=t4:
                           e.copy(out=sqrow.ap()[0:1, tt * 512:(tt + 1) * 512],
                                  in_=PX.ap()[0:1, 0:512]))
        k.wait("tensor", "act", a_evr)
        for i in range(NT):
            k.emit("tensor", lambda e, ii=i:
                   e.matmul(PX.ap()[:, 64 + ii: 65 + ii],
                            sqsq.ap()[0:64, ii * 128:(ii + 1) * 128],
                            ones64.ap(), start=True, stop=True),
                   inc=("pe", 1) if i == NT - 1 else None)
        p_sqt = k.cnt["pe"]
        k.wait("scalar", "pe", p_sqt)
        a_sqt = k.emit("scalar", lambda e: e.copy(out=sqT.ap(),
                                                  in_=PX.ap()[:, 64:80]))
        k.wait("vector", "act", a_sqt)
        d_sqte = k.emit("vector", lambda e: e.tensor_scalar_add(
            out=sqTe.ap(), in0=sqT.ap(), scalar1=1e-6))
        k.wait("tensor", "act", a_qk)
        for t4 in range(4):
            k.emit("tensor", lambda e, tt=t4:
                   e.matmul(PL.ap()[:, tt * 512:(tt + 1) * 512],
                            k.t["onesrow"].ap(),
                            sqrow.ap()[0:1, tt * 512:(tt + 1) * 512],
                            start=True, stop=True),
                   inc=("pe", 1) if t4 == 3 else None)
        p_sqjb = k.cnt["pe"]
        k.wait("scalar", "pe", p_sqjb)
        a_sqjb = k.emit("scalar", lambda e: e.copy(out=sqjb.ap(), in_=PL.ap()))
        k.wait("vector", "pe", oneh_pe)
        d_oneh = k.emit("vector", lambda e: e.memset(oneh.ap(), 0))
        k.wait("gpsimd", "dve", d_oneh)
        k.dma("gpsimd", oneh.ap()[h:h + 1, :], k.t["onesrow_bf"].ap(), "g")
        g_oneh = k.cnt["g"]
        k.wait("tensor", "g", g_oneh)
        k.wait("tensor", "act", a_sqjb)
        for t4 in range(4):
            k.emit("tensor", lambda e, tt=t4:
                   e.matmul(PL.ap()[:, tt * 512:(tt + 1) * 512],
                            oneh.ap(),
                            massT.ap()[0:16, tt * 512:(tt + 1) * 512],
                            start=True, stop=True),
                   inc=("pe", 1) if t4 == 3 else None)
        p_mjb = k.cnt["pe"]
        oneh_pe = p_mjb
        k.wait("scalar", "pe", p_mjb)
        a_mjb = k.emit("scalar", lambda e: e.copy(out=mjb.ap(), in_=PL.ap()))
        a_prev_exp = a_mjb
        d_pl_read = k.cnt["dve"]
        for i in range(NT):
            k.wait("tensor", "act", a_prev_exp)
            k.wait("tensor", "dve", d_pl_read)
            for kc in range(4):
                k.emit("tensor", lambda e, kk=kc,
                       l=qk_sb.ap()[0:64, i * 128:(i + 1) * 128],
                       r=qk_sb.ap()[0:64, kc * 512:(kc + 1) * 512]:
                       e.matmul(PL.ap()[:, kk * 512:(kk + 1) * 512], l, r,
                                start=True, stop=True),
                       inc=("pe", 1) if kc == 3 else None)
            p_dots = k.cnt["pe"]
            k.wait("vector", "pe", p_dots)
            k.wait("vector", "act", a_prev_exp)
            k.emit("vector", lambda e: e.scalar_tensor_tensor(
                out=wb.ap(), in0=PL.ap(), scalar=-2.0, in1=sqjb.ap(),
                op0=ALU.mult, op1=ALU.add))
            d_wb1 = k.cnt["dve"]
            d_pl_read = d_wb1
            k.wait("scalar", "dve", d_wb1)
            a_wb2 = k.emit("scalar", lambda e, ii=i: e.activation(
                out=wb.ap(), in_=wb.ap(), func=AF.Identity,
                bias=sqTe.ap()[:, ii:ii + 1], scale=1.0))
            k.wait("vector", "act", a_wb2)
            k.emit("vector", lambda e: e.tensor_scalar_max(
                out=wb.ap(), in0=wb.ap(), scalar1=1e-6))
            k.emit("vector", lambda e: e.reciprocal(out=wb.ap(), in_=wb.ap()))
            k.emit("vector", lambda e: e.tensor_mul(out=wb.ap(), in0=wb.ap(),
                                                    in1=mjb.ap()))
            k.emit("vector", lambda e: e.reduce_max(out=nm.ap(), in_=wb.ap(),
                                                    axis=AX.X))
            d_nm2 = k.emit("vector", lambda e, ii=i: e.tensor_scalar(
                out=nm2.ap(), in0=nm.ap(),
                scalar1=mass.ap()[:, ii * 16 + h: ii * 16 + h + 1],
                scalar2=-1.0, op0=ALU.mult, op1=ALU.mult))
            k.wait("scalar", "dve", d_nm2)
            a_exp = k.emit("scalar", lambda e, ii=i: e.activation(
                out=E.ap(), in_=wb.ap(), func=AF.Exp,
                bias=nm2.ap(),
                scale=mass.ap()[:, ii * 16 + h: ii * 16 + h + 1],
                accum_out=ssum.ap()))
            a_prev_exp = a_exp
            k.wait("vector", "act", a_exp)
            k.emit("vector", lambda e: e.reciprocal(out=rinv.ap(), in_=ssum.ap()))
            d_esc = k.emit("vector", lambda e: e.tensor_scalar_mul(
                out=E.ap(), in0=E.ap(), scalar1=rinv.ap()))
            k.wait("tensor", "dve", d_esc)
            for j0 in (0, 8):
                k.wait("tensor", "act", a_prev_ptevac)
                for j in range(8):
                    k.emit("tensor", lambda e,
                           src=E.ap()[:, (j0 + j) * 128:(j0 + j + 1) * 128],
                           dst=PT.ap()[:, j * 128:(j + 1) * 128]:
                           e.transpose(dst, src, k.t["ident_bf"].ap()),
                           inc=("pe", 1) if j == 7 else None)
                k.wait("scalar", "pe", k.cnt["pe"])
                a_prev_ptevac = k.emit("scalar", lambda e, jj=j0:
                                       e.copy(out=ET.ap()[:, jj * 128:
                                                          (jj + 8) * 128],
                                              in_=PT.ap()))
            bank = PA if (h * NT + i) % 2 == 0 else PB
            k.wait("tensor", "act", a_prev_ptevac)
            k.wait("tensor", "dve", d_radd)
            for j in range(NT):
                k.emit("tensor", lambda e,
                       l=ET.ap()[:, j * 128:(j + 1) * 128],
                       r=v_sb.ap()[:, j * 1024 + h * 64: j * 1024 + h * 64 + 64],
                       o=bank.ap()[:, 0:64], st=(j == 0), sp=(j == NT - 1):
                       e.matmul(o, l, r, start=st, stop=sp),
                       inc=("pe", 1) if j == NT - 1 else None)
            p_av = k.cnt["pe"]
            k.wait("vector", "pe", p_av)
            d_radd = k.emit("vector", lambda e, b=bank,
                            d=R.ap()[:, i * 1024 + h * 64:
                                     i * 1024 + h * 64 + 64]:
                            e.tensor_add(out=d, in0=d, in1=b.ap()[:, 0:64]))
    return d_radd



def alloc_l3_ffn(k):
    k.arena_reset()
    k.sba("l3f_h", [128, 16 * 1024], BF16, at=0)
    k.sba("l3f_dn", [128, 16 * 1024], BF16, at=32 * 1024)
    k.sba("l3f_sg", [128, 1024], F32, at=64 * 1024)
    k.sba("l3f_ug0", [128, 2048], BF16, at=68 * 1024)
    k.sba("l3f_ug1", [128, 2048], BF16, at=72 * 1024)
    k.sba("l3f_g", [128, 128], F32, at=76 * 1024)
    k.sba("l3f_wrbf", [128, 64], BF16, at=76 * 1024 + 512)
    k.sba("l3f_t1", [128, 1], F32, at=76 * 1024 + 640)


def emit_l3_ffn(k, aps, xn_act):
    xnT, R = k.t["xnT"], k.t["R"]
    stage = k.t["wstage"]
    hbf, dnbf, sg = k.t["l3f_h"], k.t["l3f_dn"], k.t["l3f_sg"]
    ug = (k.t["l3f_ug0"], k.t["l3f_ug1"])
    g_all, wrbf, t1 = k.t["l3f_g"], k.t["l3f_wrbf"], k.t["l3f_t1"]
    lg, prE, msk = k.t["st_sum"], k.t["st_sq"], k.t["st_m2"]
    nm, ssum, rinv = k.t["sm_nm"], k.t["sm_sum"], k.t["sm_rinv"]
    PL, PA, PB, PX = k.t["PL"], k.t["PA"], k.t["PB"], k.t["PX"]
    one = k.t["c_one"]

    k.wait("sync", "act", xn_act)
    k.wait("gpsimd", "act", xn_act)
    for c in range(NC):
        k.dma("sync", stage.ap()[:, c * 8:(c + 1) * 8], aps["l3f_wr"][c])
    k.wait("gpsimd", "w", k.cnt["w"])
    po_wr = k.emit("gpsimd", lambda e: e.tensor_copy(out=wrbf.ap(),
                                                     in_=stage.ap()[:, 0:64]))

    k.wait("tensor", "pool", po_wr)
    k.wait("tensor", "act", xn_act)
    d_g = 0
    for i in range(NT):
        if d_g:
            k.wait("tensor", "dve", d_g)
        for c in range(NC):
            k.emit("tensor", lambda e,
                   l=xnT.ap()[:, c * T + i * 128: c * T + (i + 1) * 128],
                   r=wrbf.ap()[:, c * 8:(c + 1) * 8],
                   o=PX.ap()[:, 0:8], st=(c == 0), sp=(c == NC - 1):
                   e.matmul(o, l, r, start=st, stop=sp),
                   inc=("pe", 1) if c == NC - 1 else None)
        p_rt = k.cnt["pe"]
        k.wait("vector", "pe", p_rt)
        k.emit("vector", lambda e: e.tensor_copy(out=lg.ap()[:, 0:8],
                                                 in_=PX.ap()[:, 0:8]))
        d_lg = k.emit("vector", lambda e: e.reduce_max(out=nm.ap(),
                                                       in_=lg.ap()[:, 0:8],
                                                       axis=AX.X, negate=True))
        k.wait("scalar", "dve", d_lg)
        a_exp = k.emit("scalar", lambda e: e.activation(
            out=prE.ap()[:, 0:8], in_=lg.ap()[:, 0:8], func=AF.Exp,
            bias=nm.ap(), scale=1.0, accum_out=ssum.ap()))
        k.wait("vector", "act", a_exp)
        k.emit("vector", lambda e: e.reciprocal(out=rinv.ap(), in_=ssum.ap()))
        k.emit("vector", lambda e: e.tensor_scalar_mul(out=prE.ap()[:, 0:8],
                                                       in0=prE.ap()[:, 0:8],
                                                       scalar1=rinv.ap()))
        k.emit("vector", lambda e: e.reduce_max(out=t1.ap(), in_=prE.ap()[:, 0:8],
                                                axis=AX.X))
        d_d = k.emit("vector", lambda e: e.tensor_scalar(
            out=msk.ap()[:, 0:8], in0=prE.ap()[:, 0:8], scalar1=t1.ap(),
            scalar2=None, op0=ALU.subtract))
        k.wait("scalar", "dve", d_d)
        a_m = k.emit("scalar", lambda e: e.activation(
            out=msk.ap()[:, 0:8], in_=msk.ap()[:, 0:8], func=AF.Relu,
            bias=one.ap(), scale=BIG))
        k.wait("vector", "act", a_m)
        d_g = k.emit("vector", lambda e, o=g_all.ap()[:, i * 8:(i + 1) * 8]:
                     e.tensor_scalar(out=o, in0=msk.ap()[:, 0:8], scalar1=t1.ap(),
                                     scalar2=2.0, op0=ALU.mult, op1=ALU.mult))

    d_epi = d_g
    d_h = 0
    down_end_pe = 0
    ug_user_pe = {}
    po_ug = {}
    po_dn = 0
    for ex in range(8):
        for blk in range(2):
            for half in range(2):
                for cc in range(8):
                    k.wait("sync", "pool", k.cnt["pool"])
                    k.dma("sync", stage.ap()[:, 0:1024],
                          aps["l3f_dn"][ex][half * 16 + 2 * cc])
                    k.dma("sync", stage.ap()[:, 1024:2048],
                          aps["l3f_dn"][ex][half * 16 + 2 * cc + 1])
                    k.wait("gpsimd", "w", k.cnt["w"])
                    if cc == 0:
                        k.wait("gpsimd", "pe", down_end_pe)
                    k.emit("gpsimd", lambda e, c2=cc: e.tensor_copy(
                        out=dnbf.ap()[:, c2 * 2048:(c2 + 1) * 2048],
                        in_=stage.ap()))
                po_dn = k.cnt["pool"]
                k.wait("vector", "pe", down_end_pe)
                for m in range(16):
                    idx = ((ex * 2 + blk) * 2 + half) * 16 + m
                    buf = idx % 2
                    k.wait("sync", "pool", k.cnt["pool"])
                    k.dma("sync",
                          stage.ap().rearrange("p (c j) -> p c j", c=NC),
                          aps["l3f_ug"][ex][half * 16 + m])
                    k.wait("gpsimd", "w", k.cnt["w"])
                    if idx >= 2:
                        k.wait("gpsimd", "pe", ug_user_pe[idx - 2])
                    po_ug[idx] = k.emit("gpsimd", lambda e, b=buf: e.tensor_copy(
                        out=ug[b].ap(), in_=stage.ap()))
                    k.wait("tensor", "pool", po_ug[idx])
                    k.wait("tensor", "dve", d_h)
                    for n2 in range(2):
                        for c in range(NC):
                            k.emit("tensor", lambda e,
                                   l=ug[buf].ap()[:, c * 256: c * 256 + 128],
                                   r=xnT.ap()[:, c * T + blk * 1024 + n2 * 512:
                                              c * T + blk * 1024 + (n2 + 1) * 512],
                                   o=PL.ap()[:, n2 * 512:(n2 + 1) * 512],
                                   st=(c == 0), sp=(c == NC - 1):
                                   e.matmul(o, l, r, start=st, stop=sp))
                            k.emit("tensor", lambda e,
                                   l=ug[buf].ap()[:, c * 256 + 128: c * 256 + 256],
                                   r=xnT.ap()[:, c * T + blk * 1024 + n2 * 512:
                                              c * T + blk * 1024 + (n2 + 1) * 512],
                                   o=PL.ap()[:, 1024 + n2 * 512: 1024 + (n2 + 1) * 512],
                                   st=(c == 0), sp=(c == NC - 1):
                                   e.matmul(o, l, r, start=st, stop=sp),
                                   inc=("pe", 1) if (n2 == 1 and c == NC - 1) else None)
                    ug_user_pe[idx] = k.cnt["pe"]
                    p_h = k.cnt["pe"]
                    k.wait("scalar", "pe", p_h)
                    k.wait("scalar", "dve", d_h)
                    a_silu = k.emit("scalar", lambda e: e.activation(
                        out=sg.ap(), in_=PL.ap()[:, 1024:2048], func=AF.Silu))
                    k.wait("vector", "act", a_silu)
                    d_h = k.emit("vector", lambda e, mm=m:
                                 e.tensor_mul(out=hbf.ap()[:, mm * 1024:(mm + 1) * 1024],
                                              in0=sg.ap(), in1=PL.ap()[:, 0:1024]))
                k.wait("tensor", "dve", d_h)
                k.wait("tensor", "pool", po_dn)
                for t in range(8):
                    i = blk * 8 + t
                    k.wait("tensor", "dve", d_epi)
                    for n in range(2):
                        for cp in range(16):
                            k.emit("tensor", lambda e,
                                   o=(PA.ap() if n == 0 else PB.ap()),
                                   l=hbf.ap()[:, cp * 1024 + t * 128:
                                              cp * 1024 + (t + 1) * 128],
                                   r=dnbf.ap()[:, cp * 1024 + n * 512:
                                               cp * 1024 + (n + 1) * 512],
                                   st=(cp == 0), sp=(cp == 15):
                                   e.matmul(o, l, r, start=st, stop=sp),
                                   inc=("pe", 1) if (n == 1 and cp == 15) else None)
                    p_dn = k.cnt["pe"]
                    k.wait("vector", "pe", p_dn)
                    if ex < 7:
                        k.emit("vector", lambda e, i2=i, e2=ex,
                               d=R.ap()[:, i * 1024: i * 1024 + 512]:
                               e.scalar_tensor_tensor(
                                   out=d, in0=PA.ap(),
                                   scalar=g_all.ap()[:, i2 * 8 + e2: i2 * 8 + e2 + 1],
                                   in1=d, op0=ALU.mult, op1=ALU.add))
                        d_epi = k.emit("vector", lambda e, i2=i, e2=ex,
                                       d=R.ap()[:, i * 1024 + 512: (i + 1) * 1024]:
                                       e.scalar_tensor_tensor(
                                           out=d, in0=PB.ap(),
                                           scalar=g_all.ap()[:, i2 * 8 + e2:
                                                             i2 * 8 + e2 + 1],
                                           in1=d, op0=ALU.mult, op1=ALU.add))
                    else:
                        k.emit("vector", lambda e,
                               d=R.ap()[:, i * 1024: i * 1024 + 512]:
                               e.tensor_add(out=d, in0=d, in1=PA.ap()))
                        d_epi = k.emit("vector", lambda e,
                                       d=R.ap()[:, i * 1024 + 512: (i + 1) * 1024]:
                                       e.tensor_add(out=d, in0=d, in1=PB.ap()))
                down_end_pe = k.cnt["pe"]
    return d_epi



def build(layers=("l0_attn",)):
    nc = bass.Bass("TRN2", target_bir_lowering=False)
    k = K(nc)
    aps = {}
    aps["x"] = nc.dram_tensor("x", [T, D], F32, kind="ExternalInput").ap()
    aps["out"] = nc.dram_tensor("out", [T, D], F32, kind="ExternalOutput").ap()
    for nm, shape, dt in (("c_ident_bf", [128, 128], BF16),
                          ("c_ident_f", [128, 128], F32),
                          ("c_causal", [128, 128], F32),
                          ("c_onesrow", [1, 128], F32)):
        aps[nm] = nc.dram_tensor(nm, shape, dt, kind="ExternalInput").ap()
    WSPECS = {
        "l0_attn": (("l0_wq", [8, 128, 512], F32), ("l0_wk", [8, 128, 512], F32),
                    ("l0_wv", [8, 128, 8192], F32)),
        "l0_ffn": (("l0f_wg", [8, 128, 16], F32), ("l0f_bg", [1, 16], F32),
                   ("l0f_bias", [1, 16], F32), ("l0f_w1", [16, 8, 128, 2048], F32),
                   ("l0f_w2", [16, 16, 128, 1024], F32), ("l0f_b1fm", [128, 256], F32),
                   ("l0f_b2", [16, 1024], F32)),
        "l1_attn": (("l1_wv", [8, 128, 1024], F32), ("l1_wqk", [16, 8, 128, 128], F32),
                    ("l1_bqk", [64, 32], F32), ("l1_bv", [1, 1024], F32),
                    ("l1_wp", [8, 128, 1024], F32), ("l1_bp", [1, 1024], F32)),
        "l1_ffn": (("l1f_wr", [8, 128, 1024], F32), ("l1f_wk", [8, 128, 1024], F32),
                   ("l1f_wv", [8, 128, 1024], F32), ("l1f_mur", [128, 8], F32),
                   ("l1f_muk", [128, 8], F32)),
        "l2_attn": (("l2_wv", [8, 128, 1024], F32),
                    ("l2_wqk", [16, 8, 128, 128], F32),
                    ("l2_wp", [8, 128, 1024], F32)),
        "l2_ffn": (("l2f_wgu", [24, 8, 128, 256], I32),
                   ("l2f_wd", [24, 128, 1024], I32)),
        "l3_attn": (("l3_wqk", [8, 128, 1024], F32), ("l3_wv", [8, 128, 1024], F32),
                    ("l3_bqk", [64, 16], F32), ("l3_bv", [1, 1024], F32),
                    ("l3_wmass_flat", [128, 128], F32), ("l3_bm", [1, 16], F32)),
        "l3_ffn": (("l3f_wr", [8, 128, 8], F32),
                   ("l3f_ug", [8, 32, 128, 8, 256], F32),
                   ("l3f_dn", [8, 32, 128, 1024], F32)),
    }
    for lyr in layers:
        for nm, shape, dt in WSPECS.get(lyr, ()):
            aps[nm] = nc.dram_tensor(nm, shape, dt, kind="ExternalInput").ap()

    alloc_common(k)

    d0 = emit_prologue(k, aps)
    r_ready = d0
    for lyr in layers:
        if lyr == "nop":
            continue
        xn_act = emit_ln(k, r_ready)
        if lyr == "ln_only":
            k.wait("vector", "act", xn_act)
            r_ready = k.emit("vector", lambda e: e.tensor_copy(
                out=k.t["R"].ap()[:, 0:1], in_=k.t["R"].ap()[:, 0:1]))
        if lyr == "l0_attn":
            alloc_l0_attn(k)
            r_ready = emit_l0_attn(k, aps, xn_act)
        if lyr == "l0_ffn":
            alloc_l0_ffn(k)
            r_ready = emit_l0_ffn(k, aps, xn_act)
        if lyr == "l1_attn":
            alloc_mha(k, "l1")
            r_ready = emit_mha(k, aps, xn_act, "l1", True, True)
        if lyr == "l2_attn":
            alloc_mha(k, "l2")
            r_ready = emit_mha(k, aps, xn_act, "l2", False, False)
        if lyr == "l2_ffn":
            alloc_l2_ffn(k)
            r_ready = emit_l2_ffn(k, aps, xn_act)
        if lyr == "l3_attn":
            alloc_l3_attn(k)
            r_ready = emit_l3_attn(k, aps, xn_act)
        if lyr == "l3_ffn":
            alloc_l3_ffn(k)
            r_ready = emit_l3_ffn(k, aps, xn_act)
        if lyr == "l1_ffn":
            alloc_l1_ffn(k)
            r_ready = emit_l1_ffn(k, aps, xn_act)
    emit_epilogue(k, aps, r_ready)
    return nc, k


def kernel(**inputs):
    w = prep_weights(inputs)
    xs = prep_x(inputs)
    with contextlib.ExitStack() as es:
        nc, k = build(ALL_LAYERS)
        k.replay(es)
        declared = {a.memorylocations[0].name
                    for a in nc.m.functions[0].allocations
                    if isinstance(a, mybir.MemoryLocationSet)}
        in_maps = []
        for b in range(8):
            m = {"x": xs[b]}
            m.update({kk: v for kk, v in w.items() if kk in declared})
            in_maps.append(m)
        res = run_bass_kernel_spmd(nc, in_maps, core_ids=list(range(8)))
    return np.stack([r["out"] for r in res.results], axis=0)
